# version 23
# baseline (speedup 1.0000x reference)
import numpy as np
import ml_dtypes
import concourse.bass as bass
import concourse.mybir as mybir
from concourse.bass_utils import run_bass_kernel_spmd

F32 = mybir.dt.float32
BF16 = mybir.dt.bfloat16
I32 = mybir.dt.int32
AF = mybir.ActivationFunctionType
ALU = mybir.AluOpType
AX = mybir.AxisListType
BFNP = ml_dtypes.bfloat16

B, T, D = 16, 2048, 1024
H, KD, VD, MD, DFF = 4, 256, 256, 64, 4096
NCORES = 8
SEQ = B // NCORES
NT = SEQ * T
CA = 512
NCHA = NT // CA
CHPS = T // CA
CB = 256
NCHB = NT // CB
GLN = 8.0
EPS = 1e-5
LNS = float(np.log(KD ** -0.5))
PI = float(np.pi)
MAGIC = 12582912.0
MAGICR = float(np.frombuffer(np.uint32(0x5F3759DF).tobytes(), np.float32)[0])
STATE_EN = True
STATE_FORCE = False
TE_SELFWAIT_DROP = True
CHAIN_LAG = 2
DROP_STAGES = {"st2", "st3", "ktm", "st4", "st5", "st8", "pass1", "pass2w", "pass2o", "state"}

ENGINES = ("sync", "gpsimd", "tensor", "vector", "scalar")
DMA_ENG = {"sync", "gpsimd"}
SEM_POOL = {"sync": ["sy%d" % i for i in range(8)],
            "gpsimd": ["gp%d" % i for i in range(4)],
            "tensor": ["te"], "vector": ["ve"], "scalar": ["sc"],
            "pool": ["gpc"]}
ALL_SEMS = [s for v in SEM_POOL.values() for s in v]


class Plan:
    def __init__(self):
        self.ops = {e: [] for e in ENGINES}
        self.nop = {e: 0 for e in list(ENGINES) + ["pool"]}
        self.cnt = {s: 0 for s in ALL_SEMS}
        self.pending = {e: None for e in ENGINES}
        self.drop_te = False

    def op(self, eng, build, wait=()):
        stream = "gpsimd" if eng == "pool" else eng
        w = []
        if self.pending[stream] is not None:
            w.extend(self.pending[stream])
            self.pending[stream] = None
        pool = SEM_POOL[eng]
        if eng in DMA_ENG:
            sem = pool[self.nop[eng] % len(pool)]
            if self.cnt[sem] > 0:
                w.append((sem, self.cnt[sem]))
            step = 16
        else:
            sem = pool[0]
            step = 1
            if eng in ("vector", "scalar", "pool") and self.cnt[sem] > 0:
                w.append((sem, max(1, self.cnt[sem] - (CHAIN_LAG - 1))))
        flat = []

        def _flat(t):
            if t is None:
                return
            if isinstance(t, list):
                for x in t:
                    _flat(x)
            else:
                flat.append(t)
        for t in wait:
            _flat(t)
        for se, sv in flat:
            if self.drop_te and eng == "tensor" and se == "te":
                continue
            w.append((se, sv))
        self.nop[eng] += 1
        self.cnt[sem] += step
        self.ops[stream].append((build, tuple(w), sem, step))
        return (sem, self.cnt[sem])

    def snapshot(self):
        return [(s, self.cnt[s]) for s in ALL_SEMS if self.cnt[s] > 0]

    def set_pending(self, snap):
        for e in ENGINES:
            self.pending[e] = list(snap)

    def barrier(self):
        self.set_pending(self.snapshot())

    def replay(self, eng, proxy, sems):
        hw = {}
        for build, waits, sem, step in self.ops[eng]:
            for (se, sv) in waits:
                if hw.get(se, -1) < sv:
                    proxy.wait_ge(sems[se], sv)
                    hw[se] = sv
            build(proxy).then_inc(sems[sem], step)

    def final_wait(self, eng, proxy, sems):
        for s in ALL_SEMS:
            if self.cnt[s] > 0:
                proxy.wait_ge(sems[s], self.cnt[s])


def build_program(debug=False):
    nc = bass.Bass("TRN2", target_bir_lowering=False)

    x_h = nc.dram_tensor("x", [SEQ, T, D], F32, kind="ExternalInput")
    wq_h = nc.dram_tensor("wq", [D, D], F32, kind="ExternalInput")
    wk_h = nc.dram_tensor("wk", [D, D], F32, kind="ExternalInput")
    wv_h = nc.dram_tensor("wv", [D, D], F32, kind="ExternalInput")
    wf_h = nc.dram_tensor("wf", [D, H * MD], F32, kind="ExternalInput")
    gnw_h = nc.dram_tensor("gnw", [VD], F32, kind="ExternalInput")
    wo_h = nc.dram_tensor("wo", [D, D], F32, kind="ExternalInput")
    w1_h = nc.dram_tensor("w1", [D, DFF], F32, kind="ExternalInput")
    b1_h = nc.dram_tensor("b1", [DFF], F32, kind="ExternalInput")
    qph_h = nc.dram_tensor("qph", [DFF], F32, kind="ExternalInput")
    qwr_h = nc.dram_tensor("qwr", [DFF], F32, kind="ExternalInput")
    qwi_h = nc.dram_tensor("qwi", [DFF], F32, kind="ExternalInput")
    w2_h = nc.dram_tensor("w2", [DFF, D], F32, kind="ExternalInput")
    b2_h = nc.dram_tensor("b2", [D], F32, kind="ExternalInput")
    ctri_h = nc.dram_tensor("ctri", [128, 128], F32, kind="ExternalInput")
    cidb_h = nc.dram_tensor("cidb", [128, 128], BF16, kind="ExternalInput")
    cidf_h = nc.dram_tensor("cidf", [128, 128], F32, kind="ExternalInput")
    cones_h = nc.dram_tensor("cones", [128, 128], F32, kind="ExternalInput")
    co1b_h = nc.dram_tensor("cones1b", [1, 512], BF16, kind="ExternalInput")
    cbias_h = nc.dram_tensor("cbias", [128, 8], F32, kind="ExternalInput")
    y_h = nc.dram_tensor("y", [SEQ, T, D], F32, kind="ExternalOutput")
    x1tm_h = nc.dram_tensor("x1tm", [NT, D], F32, kind="Internal")
    x1fm_h = nc.dram_tensor("x1fm", [D, NT], BF16, kind="Internal")
    ebc_h = nc.dram_tensor("ebcscr", [256], F32, kind="Internal")

    dbg = {}
    if debug:
        for nm, shp in [("dq", [128, 4096]), ("dk", [128, 4096]),
                        ("dv", [128, 4096]), ("df", [128, 1024]),
                        ("ds", [128, 1024]), ("deb", [128, 1024]),
                        ("dotm", [128, 4096]), ("dx1", [128, 1024]),
                        ("dhid", [128, 256]), ("dktm", [128, 4096])]:
            dbg[nm] = nc.dram_tensor(nm, shp, F32, kind="ExternalOutput")
        dbg["dP"] = nc.dram_tensor("dP", [128, 1024], BF16, kind="ExternalOutput")
        dbg["dact"] = nc.dram_tensor("dact", [128, 8192], BF16, kind="ExternalOutput")

    AW = nc.alloc_sbuf_tensor("AW", [128, 34816], BF16)
    off = {}
    cur = [0]

    def al(name, elems):
        cur[0] = (cur[0] + 15) // 16 * 16
        off[name] = cur[0]
        cur[0] += elems

    for nm, n in [("q_fm", 4096), ("k_fm", 4096), ("k_tm", 4096),
                  ("v_tm", 4096), ("o_tm", 4096),
                  ("f", 2048), ("eb", 2048), ("ebs", 2048),
                  ("enb", 2048), ("P", 1024), ("tmpa", 1024), ("tmpb", 1024),
                  ("o_fmt", 2048), ("x1pt", 2048), ("x1bt", 1024),
                  ("x1fmt", 1024), ("x1tmt", 2048), ("pt2", 512),
                  ("pt2T", 1024), ("PT", 1024), ("spm", 256), ("wtm", 256),
                  ("lbuf", 512), ("e1b", 512), ("bcb", 128), ("ebcr", 512),
                  ("ebcc", 16), ("hk0", 1024), ("hk1", 1024), ("hkb", 512),
                  ("hv0", 2048), ("hv1", 2048), ("hvb", 1024)]:
        al(nm, n)
    zone_a_end = max(cur[0], 50192)
    off["W2"], off["ACT"] = 0, 32768
    off["b1row"], off["b2row"] = 40960, 45056
    off["ypt"], off["yt"] = 46080, 48128
    cur[0] = zone_a_end
    for nm, n in [("x_tm", 8192), ("xb", 4096), ("x_fm", 4096), ("cols", 128),
                  ("b1c", 192),
                  ("gcols", 16), ("d1c", 64), ("d2c", 64), ("ctri", 256),
                  ("cidb", 128), ("cidf", 256), ("cones", 256), ("co1b", 512),
                  ("cbias", 16)]:
        al(nm, n)
    AC = nc.alloc_sbuf_tensor("AC", [128, (cur[0] + 15) // 16 * 16], BF16)
    ACF = AC.bitcast(F32)
    ACI = AC.bitcast(I32)

    def coli(i):
        o = off["cols"] // 2 + i
        return ACI[0:128, o:o + 1]

    def emit_rsqrt2(OP, dsti, srci, scri, n, wait=()):
        ACIc = lambda i, n: ACI[0:128, off["cols"] // 2 + i:
                                off["cols"] // 2 + i + n]
        MGI = ACI[0:128, off["cbias"] // 2 + 4: off["cbias"] // 2 + 5]
        tk = OP("vector", lambda v: v.tensor_scalar(
            ACIc(dsti, n), ACIc(srci, n), 1, None, ALU.logical_shift_right),
            wait=list(wait))
        tk = OP("vector", lambda v: v.tensor_tensor(
            ACIc(dsti, n), MGI.broadcast_to([128, n]) if n > 1 else MGI,
            ACIc(dsti, n), ALU.subtract), wait=[tk])
        for _ in range(2):
            tk = OP("vector", lambda v: v.tensor_mul(
                vf("cols", scri, n), vf("cols", dsti, n), vf("cols", dsti, n)),
                wait=[tk])
            tk = OP("vector", lambda v: v.tensor_mul(
                vf("cols", scri, n), vf("cols", scri, n), vf("cols", srci, n)),
                wait=[tk])
            tk = OP("vector", lambda v: v.tensor_scalar(
                vf("cols", scri, n), vf("cols", scri, n), -0.5, 1.5,
                ALU.mult, ALU.add), wait=[tk])
            tk = OP("vector", lambda v: v.tensor_mul(
                vf("cols", dsti, n), vf("cols", dsti, n), vf("cols", scri, n)),
                wait=[tk])
        return tk

    def vb(name, c0, n, p0=0, p1=128):
        o = off[name] + c0
        return AC[p0:p1, o:o + n]

    def vf(name, c0, n, p0=0, p1=128):
        o = off[name] // 2 + c0
        return ACF[p0:p1, o:o + n]

    x_tm = lambda c0, n: vf("x_tm", c0, n)
    xb_v = lambda c0, n: vb("xb", c0, n)
    tmpc = lambda c0, n: vf("xb", 1024 + c0, n)
    x_fm = lambda c0, n: vb("x_fm", c0, n)
    q_fm = lambda c0, n: vb("q_fm", c0, n)
    k_fm = lambda c0, n: vb("k_fm", c0, n)
    k_tm = lambda c0, n: vb("k_tm", c0, n)
    v_tm = lambda c0, n: vb("v_tm", c0, n)
    o_tmv = lambda c0, n: vb("o_tm", c0, n)
    f_v = lambda c0, n: vf("f", c0, n)
    eb_v = lambda c0, n: vf("eb", c0, n)
    ebs_v = lambda c0, n: vf("ebs", c0, n)
    enb_v = lambda c0, n: vf("enb", c0, n)
    P_v = lambda c0, n: vb("P", c0, n)
    tmpa = lambda c0, n: vf("tmpa", c0, n)
    tmpb = lambda c0, n: vf("tmpb", c0, n)
    col = lambda i: vf("cols", i, 1)
    HKg = lambda g, c0, n: vf("hk%d" % g, c0, n)
    HVg = lambda g, c0, n: vf("hv%d" % g, c0, n, 0, 64)
    HKB = lambda c0, n: vb("hkb", c0, n)
    HVB = lambda c0, n, p0=0: vb("hvb", c0, n, p0, p0 + 64)
    SPM = lambda sl: vb("spm", sl * 128, 128)
    WTMv = lambda sl: vb("wtm", sl * 128, 128)
    BCB = vf("bcb", 0, 64)
    EBCR = vf("ebcr", 0, 256, 0, 1)
    EBCC = vf("ebcc", 0, 4, 0, 64)

    PS = [nc.alloc_psum_tensor("ps%d" % i, [128, 512], F32) for i in range(7)]
    P7 = nc.alloc_psum_tensor("ps7", [128, 1024], BF16)
    PS0B = PS[0].bitcast(BF16)
    PS5B = PS[5].bitcast(BF16)
    PS1B = PS[1].bitcast(BF16)
    PS6B = PS[6].bitcast(BF16)

    nc._dbg_off = off
    pl = Plan()
    OP = pl.op

    def MM(out, lhsT, rhs, start, stop, wait=()):
        return OP("tensor", lambda t, o=out, l=lhsT, r=rhs, a=start, b=stop:
                  t.matmul(o, l, r, start=a, stop=b), wait=wait)

    def TRG(out, in_, ident, start, stop, wait=()):
        return OP("tensor", lambda t, o=out, i=in_, d=ident, a=start, b=stop:
                  t.matmul(o, i, d, start=a, stop=b, is_transpose=True),
                  wait=wait)

    tk_tri = OP("sync", lambda e: e.dma_start(out=vf("ctri", 0, 128), in_=ctri_h.ap()))
    tk_idb = OP("sync", lambda e: e.dma_start(out=vb("cidb", 0, 128), in_=cidb_h.ap()))
    tk_idf = OP("sync", lambda e: e.dma_start(out=vf("cidf", 0, 128), in_=cidf_h.ap()))
    tk_ones = OP("sync", lambda e: e.dma_start(out=vf("cones", 0, 128), in_=cones_h.ap()))
    tk_o1b = OP("sync", lambda e: e.dma_start(out=vb("co1b", 0, 512, 0, 1), in_=co1b_h.ap()))
    OP("sync", lambda e: e.dma_start(out=vf("cbias", 0, 8), in_=cbias_h.ap()))
    for _i, _val in enumerate([LNS, 1.0, EPS, PI / 2, MAGICR]):
        nc.const_aps.register(F32, float(_val), vf("cbias", _i, 1))
    TRI = vf("ctri", 0, 128)
    IDB = vb("cidb", 0, 128)
    IDF = vf("cidf", 0, 128)
    ONES = vf("cones", 0, 128)
    ONES1B = vb("co1b", 0, 512, 0, 1)

    OP("gpsimd", lambda e: e.dma_start(
        out=AW[:, 0:8192].rearrange("p (dc f) -> p dc f", dc=8), in_=wq_h.ap().rearrange("(dc p) f -> p dc f", p=128)))
    OP("gpsimd", lambda e: e.dma_start(
        out=AW[:, 8192:16384].rearrange("p (dc f) -> p dc f", dc=8), in_=wk_h.ap().rearrange("(dc p) f -> p dc f", p=128)))
    OP("gpsimd", lambda e: e.dma_start(
        out=AW[:, 16384:24576].rearrange("p (dc f) -> p dc f", dc=8), in_=wv_h.ap().rearrange("(dc p) f -> p dc f", p=128)))
    tk_wf = OP("gpsimd", lambda e: e.dma_start(
        out=AW[:, 32768:34816].rearrange("p (dc f) -> p dc f", dc=8), in_=wf_h.ap().rearrange("(dc p) f -> p dc f", p=128)))
    WQ = lambda c0, n: AW[:, c0:c0 + n]
    WK = lambda c0, n: AW[:, 8192 + c0:8192 + c0 + n]
    WV = lambda c0, n: AW[:, 16384 + c0:16384 + c0 + n]
    WOv = lambda c0, n: AW[:, 24576 + c0:24576 + c0 + n]
    WFv = lambda c0, n: AW[:, 32768 + c0:32768 + c0 + n]

    tk_g0 = OP("sync", lambda e: e.dma_start(
        out=vf("gcols", 0, 1), in_=gnw_h.ap().rearrange("(c p) -> p c", p=128)[:, 0:1]))
    tk_g1 = OP("sync", lambda e: e.dma_start(
        out=vf("gcols", 1, 1), in_=gnw_h.ap().rearrange("(c p) -> p c", p=128)[:, 1:2]))

    stg = lambda i: vf("o_tm", i * 1024, 1024)
    stg_free = [None, None]
    for hvc in range(8):
        sl = hvc % 2
        tkl = OP("sync", lambda e, hvc=hvc, sl=sl: e.dma_start(
            out=stg(sl), in_=wo_h.ap()[hvc * 128:(hvc + 1) * 128, :]),
            wait=[stg_free[sl]])
        stg_free[sl] = OP("vector", lambda v, hvc=hvc, sl=sl: v.tensor_scalar_mul(
            WOv(hvc * 1024, 1024), stg(sl), vf("gcols", hvc % 2, 1)),
            wait=[tkl, tk_g0, tk_g1])

    def trow(i):
        return ACF[0:32, off["xb"] // 2 + i * 128: off["xb"] // 2 + (i + 1) * 128]

    def trowi(i):
        return ACI[0:32, off["xb"] // 2 + i * 128: off["xb"] // 2 + (i + 1) * 128]
    tk_th = OP("sync", lambda e: e.dma_start(out=trow(0), in_=qph_h.ap().rearrange("(r p) -> r p", p=128)))
    tk_wr = OP("sync", lambda e: e.dma_start(out=trow(1), in_=qwr_h.ap().rearrange("(r p) -> r p", p=128)))
    tk_wi = OP("sync", lambda e: e.dma_start(out=trow(2), in_=qwi_h.ap().rearrange("(r p) -> r p", p=128)))
    t_ = OP("vector", lambda v: v.tensor_scalar(trow(3), trow(0), 1.0 / (2 * PI), MAGIC, ALU.mult, ALU.add), wait=[tk_th])
    t_ = OP("vector", lambda v: v.tensor_copy(trowi(3), trowi(3)), wait=[t_])
    t_ = OP("vector", lambda v: v.tensor_scalar_add(trow(3), trow(3), -MAGIC), wait=[t_])
    tra = OP("vector", lambda v: v.scalar_tensor_tensor(trow(3), trow(3), -2 * PI, trow(0), ALU.mult, ALU.add), wait=[t_])
    t_ = OP("vector", lambda v: v.tensor_scalar_add(trow(5), trow(0), PI / 2), wait=[tk_th])
    t_ = OP("vector", lambda v: v.tensor_scalar(trow(6), trow(5), 1.0 / (2 * PI), MAGIC, ALU.mult, ALU.add), wait=[t_])
    t_ = OP("vector", lambda v: v.tensor_copy(trowi(6), trowi(6)), wait=[t_])
    t_ = OP("vector", lambda v: v.tensor_scalar_add(trow(6), trow(6), -MAGIC), wait=[t_])
    trb = OP("vector", lambda v: v.scalar_tensor_tensor(trow(6), trow(6), -2 * PI, trow(5), ALU.mult, ALU.add), wait=[t_])
    ts1 = OP("scalar", lambda s: s.activation(trow(4), trow(3), AF.Sin), wait=[tra])
    ts2 = OP("scalar", lambda s: s.activation(trow(5), trow(6), AF.Sin), wait=[trb])
    t_ = OP("vector", lambda v: v.tensor_mul(trow(6), trow(5), trow(1)), wait=[ts2, tk_wr, trb])
    t2_ = OP("vector", lambda v: v.tensor_mul(trow(7), trow(4), trow(2)), wait=[ts1, tk_wi, t_])
    t_ = OP("vector", lambda v: v.tensor_add(trow(6), trow(6), trow(7)), wait=[t2_])
    t_ = OP("vector", lambda v: v.tensor_mul(trow(6), trow(6), trow(2)), wait=[t_])
    t_ = OP("vector", lambda v: v.tensor_scalar_mul(trow(6), trow(6), 0.1), wait=[t_])
    t2_ = OP("vector", lambda v: v.tensor_mul(trow(7), trow(6), trow(4)), wait=[t_])
    td = OP("vector", lambda v: v.tensor_mul(trow(6), trow(6), trow(5)), wait=[t2_])
    tt1 = TRG(PS[0][0:128, 0:32], trow(7), IDF[0:32, 0:32], True, True, wait=[td, tk_idf])
    tt2 = TRG(PS[1][0:128, 0:32], trow(6), IDF[0:32, 0:32], True, True, wait=[td])
    te_d1 = OP("vector", lambda v: v.tensor_copy(vf("d1c", 0, 32), PS[0][0:128, 0:32]), wait=[tt1])
    te_d2 = OP("vector", lambda v: v.tensor_copy(vf("d2c", 0, 32), PS[1][0:128, 0:32]), wait=[tt2])

    pl.barrier()

    pfree = {"P7": None, "PS0B": None, "PS1B": None}
    ps1_free = None
    abank_free = {0: None, 1: None}
    o_free = [None, None]
    prev_sq = prev_var = prev_x1 = prev_st1 = prev_st2 = None
    prev_pe2 = None
    prev_e1 = prev_p2a = None
    prev_softmax_done = None
    prev_state_done = []
    prev_ebc_readers = []
    prev_ebcr_dma = None
    prev_pe_tail = None
    prev_rms = None
    preload_x = []

    def sdrop(name):
        pl.drop_te = TE_SELFWAIT_DROP and (name in DROP_STAGES)

    fronts = {}
    xbpre = {}

    def emit_front(cc, ps56_free, lget):
        csq, ct0 = cc // CHPS, (cc % CHPS) * CA
        if cc not in xbpre:
            xbpre[cc] = OP("gpsimd", lambda e: e.dma_start(
                out=xb_v(0, 4096).rearrange("p (tt d) -> p tt d", tt=4),
                in_=x_h.ap()[csq, ct0:ct0 + CA, :].rearrange(
                    "(tt p) d -> p tt d", p=128)))
        sdrop("st2")
        tkb = {0: ps56_free, 1: ps56_free}
        for r in range(4):
            tb_ = r % 2
            TPB = PS5B if tb_ == 0 else PS6B
            tk_pe = None
            for dc in range(8):
                tk_pe = TRG(TPB[:, dc * 128:(dc + 1) * 128],
                            xb_v(r * 1024 + dc * 128, 128), IDB, dc == 0, dc == 7,
                            wait=[xbpre[cc], tk_idb, tkb[tb_], tk_pe])
            tkb[tb_] = OP("vector", lambda v, r=r, TPB=TPB: v.tensor_copy(
                AC[:, off["x_fm"]:off["x_fm"] + 4096].rearrange(
                    "p (dc t) -> p dc t", dc=8)[:, :, r * 128:(r + 1) * 128],
                TPB[:, :].rearrange("p (dc t) -> p dc t", dc=8)), wait=[tk_pe])
        xfm = tkb[1]
        if cc + 1 < NCHA:
            nsq_, nt0_ = (cc + 1) // CHPS, ((cc + 1) % CHPS) * CA
            xbpre[cc + 1] = OP("gpsimd", lambda e, nsq_=nsq_, nt0_=nt0_:
                               e.dma_start(
                out=xb_v(0, 4096).rearrange("p (tt d) -> p tt d", tt=4),
                in_=x_h.ap()[nsq_, nt0_:nt0_ + CA, :].rearrange(
                    "(tt p) d -> p tt d", p=128)), wait=[xfm])
        st = {"xfm": xfm, "tb1": tkb[1], "tb0": tkb[0], "i": 0,
              "qkf": None, "q_all": None, "k_all": None}

        def pump(n):
            sdrop("st3")
            if st["qkf"] is None:
                lf = lget()
                st["qkf"] = {0: [lf[0], lf[1], tkb[0]],
                             1: [lf[0], lf[1], xfm]}
            while n > 0 and st["i"] < 16:
                n -= 1
                i = st["i"]
                st["i"] += 1
                W, dst = (WQ, q_fm) if i < 8 else (WK, k_fm)
                qc = i % 8
                sl = qc % 2
                ps = PS[5] if sl == 0 else PS[6]
                tk_mm = None
                for dc in range(8):
                    tk_mm = MM(ps[:, 0:CA], W(dc * 1024 + qc * 128, 128),
                               x_fm(dc * 512, 512), dc == 0, dc == 7,
                               wait=[xfm, st["qkf"][sl]])
                ev = OP("scalar", lambda s_, ps=ps, dst=dst, qc=qc:
                        s_.activation(dst(qc * 512, 512), ps[:, 0:CA],
                                      AF.Silu),
                        wait=[tk_mm])
                st["qkf"][sl] = ev
                if i == 7:
                    st["q_all"] = [ev, st["qkf"][0]]
                if i == 15:
                    st["k_all"] = [ev, st["qkf"][0]]
        st["pump"] = pump
        return st
    for c in range(NCHA):
        sq, lc = c // CHPS, c % CHPS
        first = (lc == 0) or not STATE_EN
        t0 = lc * CA
        g_st, g_ld = c % 2, (c + 1) % 2

        if c == 0:
            preload_x = []
            for tt in range(4):
                preload_x.append(OP("sync", lambda e, tt=tt, sq=sq, t0=t0: e.dma_start(
                    out=x_tm(tt * 1024, 1024),
                    in_=x_h.ap()[sq, t0 + tt * 128: t0 + (tt + 1) * 128, :])))
        tk_x = preload_x

        if c == 0:
            fronts[0] = emit_front(0, None, lambda: [None, None])
            fronts[0]["pump"](16)
        fr = fronts.pop(c)
        tk_xfm = fr["xfm"]
        tk_q_all = fr["q_all"]
        tk_k_all = fr["k_all"]
        tbank_free = {0: pfree["P7"], 1: fr["tb1"]}
        sdrop("ktm")
        for tt in range(4):
            tb_ = tt % 2
            TPB = P7 if tb_ == 0 else PS6B
            tk_pe = None
            for kcg in range(8):
                tk_pe = TRG(TPB[:, kcg * 128:(kcg + 1) * 128],
                            k_fm(kcg * 512 + tt * 128, 128), IDB, kcg == 0, kcg == 7,
                            wait=[tk_k_all, tbank_free[tb_], tk_pe])
            tbank_free[tb_] = OP("vector", lambda v, tt=tt, TPB=TPB: v.tensor_copy(
                k_tm(tt * 1024, 1024), TPB[:, :]), wait=[tk_pe])
        tk_ktm = tbank_free[1]
        tk_ktm0 = tbank_free[0]

        sdrop("st4")
        tk_vv, tk_ff, tk_ss = {}, {}, {}
        tk_sgs = {}
        tk_vev = None
        fp_free = [[abank_free[1]], [abank_free[1]]]
        prev_tkss = None
        for tt in range(4):
            psl = tt % 2
            pf = (PS[2] if psl == 0 else PS[3])[:, 0:256]
            tk_mmv = tk_mmf = None
            for dc in range(8):
                xt = x_fm(dc * 512 + tt * 128, 128)
                for half in range(2):
                    tk_mmv = MM(PS[5 + half][:, 0:512], xt,
                                WV(dc * 1024 + half * 512, 512), dc == 0, dc == 7,
                                wait=[tk_xfm, tk_vev, tk_mmv, tk_ktm])
                tk_mmf = MM(pf, xt, WFv(dc * 256, 256), dc == 0, dc == 7,
                            wait=[tk_wf, tk_mmf] + fp_free[psl])
            OP("vector", lambda v, tt=tt: v.tensor_copy(
                v_tm(tt * 1024, 512), PS[5][:, 0:512]), wait=[tk_mmv])
            tk_vev = OP("vector", lambda v, tt=tt: v.tensor_copy(
                v_tm(tt * 1024 + 512, 512), PS[6][:, 0:512]), wait=[tk_mmv])
            tk_vv[tt] = tk_vev
            tk_sg = OP("scalar", lambda s, pf=pf, tt=tt: s.activation(
                f_v(tt * 256, 256), pf, AF.Sigmoid),
                wait=[tk_mmf, prev_tkss, prev_pe_tail])
            tk_sgs[tt] = tk_sg
            fp_free[psl] = [tk_sg]
            prev_tkss = tk_sg

        for tt in range(4):
            tk_ff[tt] = OP("scalar", lambda s, tt=tt: s.activation(
                f_v(tt * 256, 256), f_v(tt * 256, 256), AF.Ln),
                wait=[tk_sgs[tt]])

        sdrop("st5")
        tk_eb, tk_ebs, tk_enb, tk_P = {}, {}, {}, {}
        for tt in range(4):
            bs = (PS[2] if tt % 2 == 0 else PS[3])[:, 0:256]
            tk_mm = None
            for r in range(tt):
                tk_mm = MM(bs, ONES, f_v(r * 256, 256), r == 0, False,
                           wait=[tk_ff[r], tk_ones, tk_mm,
                                 tk_enb.get(tt - 2)] + fp_free[tt % 2])
            tk_bmm = MM(bs, TRI, f_v(tt * 256, 256), tt == 0, True,
                        wait=[tk_ff[tt], tk_tri, tk_mm,
                              tk_enb.get(tt - 2)] + fp_free[tt % 2])
            tk_eb[tt] = OP("scalar", lambda s, tt=tt, bs=bs: s.activation(
                eb_v(tt * 256, 256), bs, AF.Exp, scale=1.0 / GLN),
                wait=[tk_bmm, prev_ebcr_dma])
            tk_ebs[tt] = OP("scalar", lambda s, tt=tt, bs=bs: s.activation(
                ebs_v(tt * 256, 256), bs, AF.Exp, scale=1.0 / GLN,
                bias=LNS), wait=[tk_bmm])
            tk_enb[tt] = OP("scalar", lambda s, tt=tt, bs=bs: s.activation(
                enb_v(tt * 256, 256), bs, AF.Exp, scale=-1.0 / GLN),
                wait=[tk_bmm])
            t_ef = OP("scalar", lambda s, tt=tt: s.activation(
                vf("lbuf", 0, 256), f_v(tt * 256, 256), AF.Exp,
                scale=1.0 / GLN),
                wait=[tk_bmm, tk_P.get(tt - 1), prev_softmax_done])
            t_s = OP("vector", lambda v: v.tensor_scalar(
                vf("e1b", 0, 256), vf("lbuf", 0, 256), -1.0, 1.0,
                ALU.mult, ALU.add), wait=[t_ef])
            tk_P[tt] = OP("vector", lambda v, tt=tt: v.tensor_mul(
                P_v(tt * 256, 256), vf("e1b", 0, 256), enb_v(tt * 256, 256)),
                wait=[tk_enb[tt], t_s, prev_pe_tail])
        tk_ebc0 = OP("sync", lambda e: e.dma_start(
            out=ebc_h.ap(), in_=vf("eb", 768, 256, 127, 128)), wait=[tk_eb[3]])
        tk_ebcr = OP("sync", lambda e: e.dma_start(
            out=EBCR, in_=ebc_h.ap().rearrange("(a n) -> a n", a=1)),
            wait=[tk_ebc0])
        prev_ebcr_dma = tk_ebc0
        tk_ebcc = OP("gpsimd", lambda e: e.dma_start(
            out=EBCC, in_=ebc_h.ap().rearrange("(h m) -> m h", m=64),
            allow_slow_non_contiguous=True), wait=[tk_ebc0])

        sp_free = [pfree["PS0B"], ps1_free]
        l_free = [tk_enb[2], tk_enb[3]]
        wt_free = [None, None]
        dk_free = None
        dv_free = None
        tk_state_tail = []
        tk_o_ev = {}
        ssl_ctr = [0]
        wsl_ctr = [0]
        LBUF = vf("lbuf", 0, 256)
        E1F = vf("e1b", 0, 256)
        PT2A = lambda sl: vb("pt2", sl * 256, 256)
        tk_pt_ev = {}
        tk_pt2w = {0: None, 1: None}
        tk_softmax = {}
        sdrop("pass1")

        def emit_tp(tt, tk_p2):
            nonlocal_state = None
            tk_tp = None
            prev_ev = tk_pt_ev.get(tt - 1)
            sl = tt % 2
            for i in range(2):
                tk_tp = TRG(P7[:, i * 128:(i + 1) * 128],
                            PT2A(sl)[:, i * 128:(i + 1) * 128],
                            IDB, i == 0, False,
                            wait=[tk_p2, prev_ev, tk_ktm0, tk_tp])
            for i in range(2):
                tk_tp = TRG(P7[:, 256 + i * 128:256 + (i + 1) * 128],
                            P_v(tt * 256 + i * 128, 128), IDB, False, i == 1,
                            wait=[tk_P[tt], tk_tp])
            tk_pt2w[sl] = tk_tp
            tk_ev = OP("vector", lambda v, tt=tt: v.tensor_copy(
                vb("pt2T", 0, 1024).rearrange("p (pr t) -> p pr t", pr=2)[
                    :, :, tt * 128:(tt + 1) * 128],
                P7[:, 0:256].rearrange("p (pr t) -> p pr t", pr=2)), wait=[tk_tp])
            tk_ev = OP("vector", lambda v, tt=tt: v.tensor_copy(
                vb("PT", 0, 1024).rearrange("p (pr t) -> p pr t", pr=2)[
                    :, :, tt * 128:(tt + 1) * 128],
                P7[:, 256:512].rearrange("p (pr t) -> p pr t", pr=2)),
                wait=[tk_tp, tk_ev])
            tk_pt_ev[tt] = tk_ev

        tp_pend = None
        for tt in range(4):
            lsl = tt % 2
            L = (PS[2] if lsl == 0 else PS[3])[:, 0:256]
            tk_lmm = None
            started = [False]

            def lflag():
                st = not started[0]
                started[0] = True
                return st
            if not first:
                for h in range(4):
                    for kc in range(2):
                        tk_lmm = MM(L[:, h * 64:h * 64 + 64],
                                    q_fm((2 * h + kc) * 512 + tt * 128, 128),
                                    HKB((2 * h + kc) * 64, 64), lflag(), False,
                                    wait=[tk_q_all, l_free[lsl], tk_lmm] + prev_state_done)
            for h in range(4):
                for jt in range(tt + 1):
                    ssl = ssl_ctr[0] % 2
                    ssl_ctr[0] += 1
                    S = (PS[0] if ssl == 0 else PS[1])[:, 0:128]
                    tk_smm = MM(S, k_fm((2 * h) * 512 + jt * 128, 128),
                                q_fm((2 * h) * 512 + tt * 128, 128),
                                True, False,
                                wait=[tk_q_all, tk_k_all, sp_free[ssl]])
                    tk_smm = MM(S, k_fm((2 * h + 1) * 512 + jt * 128, 128),
                                q_fm((2 * h + 1) * 512 + tt * 128, 128),
                                False, True, wait=[tk_smm])
                    if jt == tt:
                        tk_spm = OP("vector", lambda v, S=S, ssl=ssl: v.tensor_mul(
                            SPM(ssl), S, TRI), wait=[tk_smm])
                    else:
                        tk_spm = OP("scalar", lambda s_, S=S, ssl=ssl: s_.activation(
                            SPM(ssl), S, AF.Copy), wait=[tk_smm])
                    sp_free[ssl] = tk_spm
                    tk_lmm = MM(L[:, h * 64:h * 64 + 64], SPM(ssl),
                                P_v(jt * 256 + h * 64, 64),
                                lflag(), (h == 3) and (jt == tt),
                                wait=[tk_spm, tk_P[jt], l_free[lsl], tk_lmm])
            if tp_pend is not None:
                emit_tp(*tp_pend)
            tk_lb = OP("vector", lambda v, L=L, tt=tt: v.tensor_mul(
                LBUF, L, ebs_v(tt * 256, 256)), wait=[tk_lmm, tk_ebs[tt]])
            l_free[lsl] = tk_lb
            tk_mx = OP("vector", lambda v: v.reduce_max(
                vf("cols", 0, 4), LBUF.rearrange("p (h m) -> p h m", h=4),
                axis=AX.X, negate=True), wait=[prev_e1, tk_lb])
            tk_sub = OP("vector", lambda v: v.tensor_add(
                E1F.rearrange("p (h m) -> p h m", h=4),
                LBUF.rearrange("p (h m) -> p h m", h=4),
                vf("cols", 0, 4).broadcast_to([128, 4, 64])), wait=[tk_mx])
            tk_e1 = OP("scalar", lambda s: s.activation(LBUF, E1F, AF.Exp),
                       wait=[tk_sub, prev_p2a])
            prev_e1 = tk_e1
            tk_sm = OP("vector", lambda v: v.reduce_sum(
                vf("cols", 4, 4), LBUF.rearrange("p (h m) -> p h m", h=4),
                axis=AX.X), wait=[tk_e1])
            tk_rc = OP("vector", lambda v: v.reciprocal(vf("cols", 8, 4),
                                                        vf("cols", 4, 4)),
                       wait=[tk_sm])
            tk_p2a = OP("vector", lambda v, tt=tt: v.tensor_mul(
                E1F, LBUF, eb_v(tt * 256, 256)), wait=[tk_e1, tk_eb[tt]])
            prev_p2a = tk_p2a
            tk_p2 = OP("vector", lambda v, tt=tt: v.tensor_mul(
                PT2A(tt % 2).rearrange("p (h m) -> p h m", h=4),
                E1F.rearrange("p (h m) -> p h m", h=4),
                vf("cols", 8, 4).broadcast_to([128, 4, 64])),
                wait=[tk_rc, tk_p2a, tk_pt2w[tt % 2]])
            tk_softmax[tt] = tk_p2
            prev_softmax_done = tk_p2
            tp_pend = (tt, tk_p2)
        emit_tp(*tp_pend)
        nxt_front = None
        if c + 1 < NCHA:
            nxt_front = emit_front(c + 1, tk_vv[3], lambda: l_free)
            fronts[c + 1] = nxt_front
        sdrop("pass2")
        obank_free = {0: [o_free[0]] + prev_state_done, 1: sp_free[0]}
        OBANK = {0: PS[4], 1: PS[0]}
        for tt in range(4):
            par = tt % 2
            for pair in range(2):
                OB = OBANK[pair][:, 0:512]
                tk_omm = None
                ostart = [True]

                def oflag():
                    st = ostart[0]
                    ostart[0] = False
                    return st
                if not first:
                    sdrop("pass2h")
                    for hh in range(2):
                        h = 2 * pair + hh
                        tk_omm = MM(OB[:, hh * 256:hh * 256 + 256],
                                    vb("pt2T", 0, 1024)[64 * hh:64 * hh + 64,
                                                        pair * 512 + tt * 128:
                                                        pair * 512 + tt * 128 + 128],
                                    HVB(h * 256, 256, 64 * hh), oflag(), False,
                                    wait=[tk_pt_ev[tt], obank_free[pair],
                                          tk_omm] + prev_state_done)
                for hh in range(2):
                    h = 2 * pair + hh
                    for jt in range(tt + 1):
                        wsl = wsl_ctr[0] % 2
                        wsl_ctr[0] += 1
                        Wp = (PS[2] if wsl == 0 else PS[3])[:, 0:128]
                        sdrop("pass2w")
                        tk_wmm = MM(Wp,
                                    vb("PT", 0, 1024)[64 * hh:64 * hh + 64,
                                                      pair * 512 + jt * 128:
                                                      pair * 512 + jt * 128 + 128],
                                    vb("pt2T", 0, 1024)[64 * hh:64 * hh + 64,
                                                        pair * 512 + tt * 128:
                                                        pair * 512 + tt * 128 + 128],
                                    True, True,
                                    wait=[tk_pt_ev[tt], wt_free[wsl],
                                          l_free[0], l_free[1]])
                        if jt == tt:
                            tk_wtm = OP("vector", lambda v, Wp=Wp, wsl=wsl: v.tensor_mul(
                                WTMv(wsl), Wp, TRI), wait=[tk_wmm])
                        else:
                            tk_wtm = OP("vector", lambda v, Wp=Wp, wsl=wsl: v.tensor_copy(
                                WTMv(wsl), Wp), wait=[tk_wmm])
                        wt_free[wsl] = tk_wtm
                        sdrop("pass2o")
                        tk_omm = MM(OB[:, hh * 256:hh * 256 + 256], WTMv(wsl),
                                    v_tm(jt * 1024 + h * 256, 256),
                                    oflag(), (hh == 1) and (jt == tt),
                                    wait=[tk_wtm, tk_vv[jt], tk_omm,
                                          obank_free[pair]
                                          if ostart[0] else None])
                tk_sq = OP("scalar", lambda s, OB=OB: s.activation(
                    tmpa(0, 512), OB, AF.Square), wait=[tk_omm, prev_rms])
                tk_ssq = OP("vector", lambda v: v.reduce_sum(
                    vf("cols", 12, 2), tmpa(0, 512).rearrange(
                        "p (g m) -> p g m", g=2), axis=AX.X), wait=[tk_sq])
                tk_ms = OP("vector", lambda v: v.tensor_scalar(
                    vf("cols", 14, 2), vf("cols", 12, 2), 1.0 / VD, EPS,
                    ALU.mult, ALU.add), wait=[tk_ssq])
                tk_rs = emit_rsqrt2(OP, 16, 14, 50, 2, wait=[tk_ms])
                tk_oe = None
                for hh in range(2):
                    h = 2 * pair + hh
                    tk_oe = OP("vector", lambda v, OB=OB, hh=hh, h=h, tt=tt:
                               v.tensor_scalar_mul(
                                   o_tmv(tt * 1024 + h * 256, 256),
                                   OB[:, hh * 256:hh * 256 + 256],
                                   vf("cols", 16 + hh, 1)), wait=[tk_rs, tk_oe])
                    tk_o_ev[(h, tt)] = tk_oe
                obank_free[pair] = tk_oe
                prev_rms = tk_oe
                if nxt_front is not None:
                    nxt_front["pump"](2)
        if nxt_front is not None:
            nxt_front["pump"](16)
        o_free = [obank_free[0], obank_free[1]]
        sp_free = [obank_free[1], sp_free[1]]
        pfree["PS0B"] = [pfree["PS0B"], obank_free[1]]
        sdrop("state")
        if STATE_EN and (lc < CHPS - 1 or STATE_FORCE):
            dkbf = sp_free[1]
            ps4f = obank_free[0]
            for h in range(4):
                tk_dk = {}
                tk_mm = None
                for kc in range(2):
                    DK = PS[1][:, kc * 64:kc * 64 + 64]
                    for jt in range(4):
                        tk_mm = MM(DK, k_tm(jt * 1024 + h * 256 + kc * 128, 128),
                                   P_v(jt * 256 + h * 64, 64),
                                   kc == 0 and jt == 0, kc == 1 and jt == 3,
                                   wait=[tk_ktm, tk_P[jt], dkbf,
                                         dk_free, tk_mm])
                    tk_dk[kc] = tk_mm
                tk_dk[0] = tk_dk[1]
                tk_bc = MM(PS[4][:, 0:64], ONES[0:1, :],
                           EBCR[0:1, h * 64: h * 64 + 64], True, True,
                           wait=[tk_ebcr, ps4f, dk_free])
                tk_bcb = OP("vector", lambda v: v.tensor_copy(BCB, PS[4][:, 0:64]),
                            wait=[tk_bc])
                tk_hkbw = None
                for kc in range(2):
                    DK = PS[1][:, kc * 64:kc * 64 + 64]
                    dst = HKg(g_st, (2 * h + kc) * 64, 64)
                    if first:
                        tk_up = OP("vector", lambda v, DK=DK, dst=dst: v.tensor_mul(
                            dst, DK, BCB), wait=[tk_bcb, tk_dk[kc]])
                        tk_dkr = tk_up
                    else:
                        tk_t1 = OP("vector", lambda v, kc=kc, h=h, DK=DK: v.tensor_add(
                            tmpb(0, 64), HKg(g_ld, (2 * h + kc) * 64, 64), DK),
                            wait=[tk_bcb, tk_dk[kc]])
                        tk_up = OP("vector", lambda v, dst=dst: v.tensor_mul(
                            dst, tmpb(0, 64), BCB), wait=[tk_t1])
                        tk_dkr = tk_t1
                    tk_hkbw = OP("vector", lambda v, kc=kc, h=h, dst=dst: v.tensor_copy(
                        HKB((2 * h + kc) * 64, 64), dst), wait=[tk_up])
                dkbf = tk_dkr
                dk_free = tk_hkbw
                tk_dv = None
                for jt in range(4):
                    tk_dv = MM(PS[4][0:64, 0:256], P_v(jt * 256 + h * 64, 64),
                               v_tm(jt * 1024 + h * 256, 256), jt == 0, jt == 3,
                               wait=[tk_P[jt], tk_vv[jt], dv_free, tk_dv,
                                     tk_bcb])
                dstv = HVg(g_st, h * 256, 256)
                if first:
                    tk_hv = OP("vector", lambda v, dstv=dstv, h=h: v.tensor_scalar_mul(
                        dstv, PS[4][0:64, 0:256], EBCC[0:64, h:h + 1]),
                        wait=[tk_dv, tk_ebcc])
                    ps4f = tk_hv
                else:
                    tk_hva = OP("vector", lambda v, h=h: v.scalar_tensor_tensor(
                        vf("tmpa", 0, 256, 0, 64), HVg(g_ld, h * 256, 256), 1.0,
                        PS[4][0:64, 0:256], ALU.mult, ALU.add),
                        wait=[tk_dv, tk_ebcc, prev_rms])
                    tk_hv = OP("vector", lambda v, dstv=dstv, h=h: v.tensor_scalar_mul(
                        dstv, vf("tmpa", 0, 256, 0, 64), EBCC[0:64, h:h + 1]),
                        wait=[tk_hva])
                    ps4f = tk_hva
                tk_hvbw = OP("vector", lambda v, dstv=dstv, h=h: v.tensor_copy(
                    HVB(h * 256, 256), dstv), wait=[tk_hv])
                if h % 2 == 1:
                    tk_hvbw = OP("vector", lambda v, dstv=dstv, h=h: v.tensor_copy(
                        HVB(h * 256, 256, 64), dstv), wait=[tk_hv, tk_hvbw])
                else:
                    tk_hvbw = OP("vector", lambda v, dstv=dstv, h=h: v.tensor_copy(
                        HVB(h * 256, 256, 64), dstv), wait=[tk_hv, tk_hvbw])
                dv_free = tk_hvbw
                tk_state_tail += [tk_hkbw, tk_hvbw]
            ps1_free = dkbf

        tk_o_all = tk_o_ev[(3, 3)]

        if not (STATE_EN and (lc < CHPS - 1 or STATE_FORCE)):
            ps1_free = sp_free[1]
        sdrop("st8")
        pbank = {"P7": P7, "PS0B": PS0B}
        prev_x1b = None
        ab_free = [l_free[0], l_free[1]]
        for tt in range(4):
            par = tt % 2
            ob_name = "P7" if par == 0 else "PS0B"
            OTB = pbank[ob_name]
            tk_pe = None
            for i in range(8):
                tk_pe = TRG(OTB[:, i * 128:(i + 1) * 128],
                            o_tmv(tt * 1024 + i * 128, 128), IDB, i == 0, i == 7,
                            wait=[tk_o_ev[(3, tt)], pfree[ob_name], tk_pe])
            tk_ofm = OP("vector", lambda v, OTB=OTB, par=par: v.tensor_copy(
                vb("o_fmt", par * 1024, 1024), OTB[:, :]),
                wait=[tk_pe, abank_free[par]])
            pfree[ob_name] = tk_ofm
            st8_ofm_last = tk_ofm
            AB = (PS[2], PS[3])
            tk_amm = {0: None, 1: None}
            for hvc in range(8):
                for half in range(2):
                    tk_amm[half] = MM(AB[half][:, 0:512],
                                      vb("o_fmt", par * 1024, 1024)[:, hvc * 128:(hvc + 1) * 128],
                                      WOv(hvc * 1024 + half * 512, 512),
                                      hvc == 0, hvc == 7,
                                      wait=[tk_ofm, tk_o_all, tk_amm[half],
                                            ab_free])
            OP("vector", lambda v, tt=tt, AB=AB: v.scalar_tensor_tensor(
                vf("x1pt", 0, 512), AB[0][:, 0:512], 1.0, x_tm(tt * 1024, 512),
                ALU.mult, ALU.add), wait=[tk_amm[0], prev_sq, prev_x1b, tk_x])
            tk_x1p = OP("vector", lambda v, tt=tt, AB=AB: v.scalar_tensor_tensor(
                vf("x1pt", 512, 512), AB[1][:, 0:512], 1.0,
                x_tm(tt * 1024 + 512, 512), ALU.mult, ALU.add),
                wait=[tk_amm[1]])
            abank_free[par] = tk_x1p
            ab_free = tk_x1p
            tk_sx = OP("vector", lambda v: v.reduce_sum(col(18), vf("x1pt", 0, 1024), axis=AX.X),
                       wait=[tk_x1p])
            tk_sq = OP("scalar", lambda s: s.activation(
                vb("x1bt", 0, 1024), vf("x1pt", 0, 1024), AF.Square,
                accum_out=col(19)),
                wait=[tk_x1p, prev_var, prev_pe2])
            prev_sq = tk_sq
            t_ = OP("vector", lambda v: v.tensor_scalar_mul(col(20), col(18), 1.0 / D), wait=[tk_sx])
            t_ = OP("vector", lambda v: v.tensor_mul(col(21), col(20), col(20)), wait=[t_])
            t2_ = OP("vector", lambda v: v.tensor_scalar_mul(col(22), col(19), 1.0 / D), wait=[tk_sq, t_])
            tk_var = OP("vector", lambda v: v.tensor_sub(col(22), col(22), col(21)), wait=[t2_])
            prev_var = tk_var
            tk_ln = OP("scalar", lambda s: s.activation(col(23), col(22), AF.Ln,
                                                        bias=EPS), wait=[tk_var, prev_x1])
            tk_rs = OP("scalar", lambda s: s.activation(col(24), col(23), AF.Exp,
                                                        scale=-0.5), wait=[tk_ln])
            tk_nm = OP("vector", lambda v: v.tensor_scalar_mul(col(25), col(20), -1.0),
                       wait=[tk_var])
            tk_nr = OP("vector", lambda v: v.tensor_mul(col(26), col(25), col(24)),
                       wait=[tk_rs, tk_nm])
            tk_x1 = OP("vector", lambda v: v.tensor_scalar(
                vf("x1tmt", 0, 1024), vf("x1pt", 0, 1024), col(24), col(26),
                ALU.mult, ALU.add), wait=[tk_nr, prev_st1])
            prev_x1 = tk_x1
            tk_x1b = OP("scalar", lambda s: s.activation(
                vb("x1bt", 0, 1024), vf("x1pt", 0, 1024), AF.Identity,
                scale=col(24), bias=col(26)),
                wait=[tk_nr, prev_pe2])
            prev_x1b = tk_x1b
            tk_st1 = OP("sync", lambda e, c=c, tt=tt: e.dma_start(
                out=x1tm_h.ap()[c * CA + tt * 128: c * CA + (tt + 1) * 128, :],
                in_=vf("x1tmt", 0, 1024)), wait=[tk_x1])
            prev_st1 = tk_st1
            x1b_name = "PS0B" if par == 0 else "P7"
            XTB = pbank[x1b_name]
            tk_pe2 = None
            for dc in range(8):
                tk_pe2 = TRG(XTB[:, dc * 128:(dc + 1) * 128],
                             vb("x1bt", 0, 1024)[:, dc * 128:(dc + 1) * 128], IDB,
                             dc == 0, dc == 7,
                             wait=[tk_x1b, pfree[x1b_name], tk_pe2])
            prev_pe2 = tk_pe2
            tk_x1f = OP("vector", lambda v, XTB=XTB: v.tensor_copy(
                vb("x1fmt", 0, 1024), XTB[:, :]), wait=[tk_pe2, prev_st2])
            pfree[x1b_name] = tk_x1f
            tk_st2 = OP("sync", lambda e, c=c, tt=tt: e.dma_start(
                out=x1fm_h.ap().rearrange("(dc p) n -> p dc n", p=128)[
                    :, :, c * CA + tt * 128: c * CA + (tt + 1) * 128],
                in_=vb("x1fmt", 0, 1024).rearrange("p (dc t) -> p dc t", dc=8)),
                wait=[tk_x1f])
            prev_st2 = tk_st2
        if c + 1 < NCHA:
            nsq, nlc = (c + 1) // CHPS, (c + 1) % CHPS
            nt0 = nlc * CA
            preload_x = []
            for tt in range(4):
                preload_x.append(OP("sync", lambda e, tt=tt, nsq=nsq, nt0=nt0:
                                    e.dma_start(
                    out=x_tm(tt * 1024, 1024),
                    in_=x_h.ap()[nsq, nt0 + tt * 128: nt0 + (tt + 1) * 128, :]),
                    wait=[abank_free[0], abank_free[1]]))
        prev_pe_tail = prev_pe2
        prev_state_done = tk_state_tail[-2:] if tk_state_tail else []
        if STATE_EN and (lc < CHPS - 1 or STATE_FORCE):
            prev_ebc_readers = [tk_bcb, tk_hv]
        else:
            prev_ebc_readers = []

        if debug and c == 0:
            OP("sync", lambda e: e.dma_start(out=dbg["df"].ap(), in_=f_v(0, 1024)), wait=[tk_ff[3]])
            OP("sync", lambda e: e.dma_start(out=dbg["deb"].ap(), in_=eb_v(0, 1024)), wait=[tk_eb[3]])
            OP("sync", lambda e: e.dma_start(out=dbg["dP"].ap(), in_=P_v(0, 1024)), wait=[tk_P[3]])

        if c == NCHA - 1:
            snap_ab = pl.snapshot()
            tk_b2pre = OP("gpsimd", lambda e: e.dma_start(
                out=vb("b2row", 0, 1024, 0, 1),
                in_=b2_h.ap().rearrange("(a f) -> a f", a=1)), wait=[prev_rms])
            w1g = []
            w1_waits = [tk_q_all, tk_k_all, tk_vv[3], abank_free[1]]
            for g in range(4):
                w1g.append(OP("gpsimd", lambda e, g=g: e.dma_start(
                    out=AW[:, g * 8192:(g + 1) * 8192].rearrange(
                        "p (dc f) -> p dc f", dc=2),
                    in_=w1_h.ap()[g * 256:(g + 1) * 256, :].rearrange(
                        "(dc p) f -> p dc f", p=128)), wait=[w1_waits[g]]))
            w2g = []
            w2_waits = [
                [l_free[0], l_free[1], tk_ktm],
                [tk_ktm, tk_o_ev[(3, 3)]],
                [st8_ofm_last, tk_eb[3], tk_P[3], prev_p2a, tk_ebcr],
                [l_free[0], l_free[1], tk_P[3], tk_pt2w[0], tk_pt2w[1],
                 prev_rms,
                 abank_free[0], st8_ofm_last],
            ]
            for gi in range(4):
                w2g.append(OP("gpsimd", lambda e, gi=gi: e.dma_start(
                    out=AC[:, gi * 8192:(gi + 1) * 8192].rearrange(
                        "p (fc d) -> p fc d", fc=8),
                    in_=w2_h.ap()[gi * 1024:(gi + 1) * 1024, :].rearrange(
                        "(fc p) d -> p fc d", p=128)), wait=w2_waits[gi]))
            pl.set_pending(snap_ab)

    pl.drop_te = TE_SELFWAIT_DROP
    W2C = lambda c0, n: vb("W2", c0, n)
    ACTB = lambda c0, n: vb("ACT", c0, n)
    B1R = lambda c0, n: vb("b1row", c0, n, 0, 1)
    B2R = lambda c0, n: vb("b2row", c0, n, 0, 1)
    YPT = vf("ypt", 0, 1024)
    YT = vf("yt", 0, 1024)
    W1v = lambda c0, n: AW[:, c0:c0 + n]
    XF = lambda c0, n: vb("x_fm", c0, n)
    X1T = lambda c0, n: vf("x_tm", c0, n)
    SIND, COSD = xb_v(0, 256), xb_v(256, 256)
    HPB, M1B = xb_v(512, 256), xb_v(768, 256)
    M2B, M3B = xb_v(1024, 256), xb_v(1280, 256)

    tk_b1l = OP("sync", lambda e: e.dma_start(
        out=vf("xb", 1024, 128, 0, 32),
        in_=b1_h.ap().rearrange("(fc p) -> fc p", p=128)))
    tk_b1t = TRG(PS[4][0:128, 0:32], vf("xb", 1024, 128, 0, 32),
                 IDF[0:32, 0:32], True, True, wait=[tk_b1l, tk_idf])
    tk_b1e = OP("vector", lambda v: v.tensor_copy(
        vf("b1c", 0, 32), PS[4][0:128, 0:32]), wait=[tk_b1t])
    tk_b1s = OP("vector", lambda v: v.tensor_scalar_mul(
        vf("b1c", 32, 32), vf("b1c", 0, 32), 0.1), wait=[tk_b1e])
    tk_b1 = OP("vector", lambda v: v.tensor_scalar_add(
        vf("b1c", 64, 32), vf("b1c", 32, 32), PI / 2), wait=[tk_b1s])
    tk_b2 = tk_b2pre
    tk_w1 = None
    tk_w2g = w2g

    hid_free = {0: None, 1: None, 2: tk_b1e}
    prev_m1 = prev_m2 = prev_m3 = None
    prev_act = {}
    prev_w2r = {}
    prev_yp = {0: [], 1: []}
    prev_hid_last = {0: None, 1: None}
    prev_x1t_readers = {0: [], 1: []}
    prev_ypt = {0: [], 1: []}

    def emit_bloads(c):
        cpar = c % 2
        xf0 = cpar * 2048
        xt0 = cpar * 2048
        t_xf = OP("sync", lambda e, c=c, xf0=xf0: e.dma_start(
            out=XF(xf0, 2048).rearrange("p (dc t) -> p dc t", dc=8),
            in_=x1fm_h.ap().rearrange("(dc p) n -> p dc n", p=128)[
                :, :, c * CB:(c + 1) * CB]), wait=[prev_hid_last[cpar]])
        t_x1 = OP("sync", lambda e, c=c, xt0=xt0: e.dma_start(
            out=X1T(xt0, 2048).rearrange("p (tt d) -> p tt d", tt=2),
            in_=x1tm_h.ap()[c * CB:(c + 1) * CB, :].rearrange(
                "(tt p) d -> p tt d", p=128)), wait=prev_x1t_readers[cpar])
        prev_x1t_readers[cpar] = []
        return t_xf, t_x1

    cur_loads = emit_bloads(0)
    for c in range(NCHB):
        sq, tloc = (c * CB) // T, (c * CB) % T
        cpar = c % 2
        xf0 = cpar * 2048
        xt0 = cpar * 2048
        tk_xf, tk_x1t = cur_loads
        ypsum = [(PS[5][:, 0:512], PS[6][:, 0:512]), (PS[0][:, 0:512], PS[1][:, 0:512])]
        tk_ymm = {}

        def emit_ypre():
            for tt2 in range(2):
                for dh in range(2):
                    tk_ymm[(tt2, dh)] = MM(ypsum[tt2][dh], ONES1B[0:1, 0:128],
                                           B2R(dh * 512, 512), True, False,
                                           wait=[tk_b2, tk_o1b] + prev_yp[tt2])
        w2pend = []
        tk_hid = {}

        def emit_w2(fc, tk_act):
            for tt2 in range(2):
                for dh in range(2):
                    tk_ymm[(tt2, dh)] = MM(
                        ypsum[tt2][dh], ACTB(fc * CB + tt2 * 128, 128),
                        W2C(fc * 1024 + dh * 512, 512), False, fc == 31,
                        wait=[tk_act, tk_w2g[fc // 8], tk_ymm[(tt2, dh)]])
            prev_w2r[fc] = tk_ymm[(1, 1)]
        for fc in range(32):
            hsl = fc % 3
            ph = PS[2 + hsl][:, 0:CB]
            tk_mm = None
            for dc in range(8):
                tk_mm = MM(ph, W1v(dc * 4096 + fc * 128, 128),
                           XF(xf0 + dc * CB, CB), dc == 0, dc == 7,
                           wait=[w1g[dc // 2], tk_xf, hid_free[hsl], tk_mm])
            tk_hid[fc] = tk_mm
            tk_sin = OP("scalar", lambda s, ph=ph, fc=fc: s.activation(
                SIND, ph, AF.Sin, scale=0.1, bias=vf("b1c", 32 + fc, 1)),
                wait=[tk_mm, tk_b1, prev_m1])
            tk_cos = OP("scalar", lambda s, ph=ph, fc=fc: s.activation(
                COSD, ph, AF.Sin, scale=0.1, bias=vf("b1c", 64 + fc, 1)),
                wait=[prev_m2, tk_b1])
            tk_hp = OP("scalar", lambda s2, ph=ph: s2.activation(
                HPB, ph, AF.Copy), wait=[tk_mm, prev_m3])
            hid_free[hsl] = tk_hp
            tk_m1 = OP("vector", lambda v, fc=fc: v.tensor_scalar_mul(
                M1B, SIND, vf("d2c", fc, 1)), wait=[tk_sin, te_d2, prev_m2])
            prev_m1 = tk_m1
            tk_m2 = OP("vector", lambda v, fc=fc: v.scalar_tensor_tensor(
                M2B, COSD, vf("d1c", fc, 1), M1B, ALU.mult, ALU.add),
                wait=[tk_cos, te_d1, tk_m1, prev_m3])
            prev_m2 = tk_m2
            tk_m3 = OP("vector", lambda v: v.tensor_add(M3B, M2B, HPB),
                       wait=[tk_hp, tk_m2, prev_act.get("last")])
            prev_m3 = tk_m3
            tk_act = OP("vector", lambda v, fc=fc: v.tensor_scalar(
                ACTB(fc * CB, CB), M3B, vf("b1c", fc, 1), 0.0,
                ALU.add, ALU.max),
                wait=[tk_m3, tk_b1, prev_w2r.get(fc)])
            prev_act["last"] = tk_act
            w2pend.append((fc, tk_act))
            if fc == 1:
                emit_ypre()
                prev_yp = {0: [], 1: []}
            if fc >= 2:
                emit_w2(*w2pend.pop(0))
        while w2pend:
            emit_w2(*w2pend.pop(0))
        prev_hid_last[cpar] = tk_hid[31]
        if c + 1 < NCHB:
            cur_loads = emit_bloads(c + 1)
        tk_ye = {}
        for tt2 in range(2):
            ya, yb = ypsum[tt2]
            tk_ya = OP("vector", lambda v, ya=ya, tt2=tt2, xt0=xt0:
                       v.scalar_tensor_tensor(
                           vf("b1row", tt2 * 1024, 512), ya, 1.0,
                           X1T(xt0 + tt2 * 1024, 512), ALU.mult, ALU.add),
                       wait=[tk_ymm[(tt2, 0)], tk_x1t] + prev_ypt[tt2])
            tk_yp = OP("vector", lambda v, yb=yb, tt2=tt2, xt0=xt0:
                       v.scalar_tensor_tensor(
                           vf("b1row", tt2 * 1024 + 512, 512), yb, 1.0,
                           X1T(xt0 + tt2 * 1024 + 512, 512), ALU.mult, ALU.add),
                       wait=[tk_ymm[(tt2, 1)], tk_ya])
            prev_yp[tt2] = [tk_yp]
            tk_ye[tt2] = tk_yp
        prev_x1t_readers[cpar].append(tk_ye[1])
        YPB = vf("b1row", 0, 2048)
        tk_sx = OP("vector", lambda v: v.reduce_sum(
            vf("cols", 28, 2), YPB.rearrange("p (g d) -> p g d", g=2),
            axis=AX.X), wait=[tk_ye[1]])
        tk_sq = {}
        for tt2 in range(2):
            tk_sq[tt2] = OP("scalar", lambda s, tt2=tt2: s.activation(
                tmpc(0, 1024), vf("b1row", tt2 * 1024, 1024), AF.Square,
                accum_out=col(30 + tt2)), wait=[tk_ye[tt2], tk_sq.get(0)])
        t_ = OP("vector", lambda v: v.tensor_scalar_mul(
            vf("cols", 32, 2), vf("cols", 28, 2), 1.0 / D), wait=[tk_sx])
        t_ = OP("vector", lambda v: v.tensor_mul(
            vf("cols", 34, 2), vf("cols", 32, 2), vf("cols", 32, 2)), wait=[t_])
        t2_ = OP("vector", lambda v: v.tensor_scalar(
            vf("cols", 36, 2), vf("cols", 30, 2), 1.0 / D, EPS,
            ALU.mult, ALU.add), wait=[tk_sq[0], tk_sq[1], t_])
        tk_vr = OP("vector", lambda v: v.tensor_sub(
            vf("cols", 36, 2), vf("cols", 36, 2), vf("cols", 34, 2)),
            wait=[t2_])
        tk_rsb = emit_rsqrt2(OP, 38, 36, 40, 2, wait=[tk_vr])
        t_ = OP("vector", lambda v: v.tensor_scalar_mul(
            vf("cols", 42, 2), vf("cols", 32, 2), -1.0), wait=[tk_vr])
        tk_nr = OP("vector", lambda v: v.tensor_mul(
            vf("cols", 42, 2), vf("cols", 42, 2), vf("cols", 38, 2)),
            wait=[tk_rsb, t_])
        for tt2 in range(2):
            tk_y = OP("vector", lambda v, tt2=tt2: v.tensor_scalar(
                vf("b1row", tt2 * 1024, 1024), vf("b1row", tt2 * 1024, 1024),
                col(38 + tt2), col(42 + tt2), ALU.mult, ALU.add),
                wait=[tk_nr])
            tk_st = OP("sync", lambda e, sq=sq, tloc=tloc, tt2=tt2:
                       e.dma_start(
                           out=y_h.ap()[sq, tloc + tt2 * 128:
                                        tloc + (tt2 + 1) * 128, :],
                           in_=vf("b1row", tt2 * 1024, 1024)), wait=[tk_y])
            prev_ypt[tt2] = [tk_st]

    import contextlib
    with contextlib.ExitStack() as stack:
        sems = {s_: stack.enter_context(nc.semaphore("sm_" + s_))
                for s_ in ALL_SEMS}
        block = stack.enter_context(nc.Block())

        @block.sync
        def _(e):
            pl.replay("sync", e, sems)
            pl.final_wait("sync", e, sems)

        @block.gpsimd
        def _(e):
            pl.replay("gpsimd", e, sems)

        @block.tensor
        def _(e):
            pl.replay("tensor", e, sems)

        @block.vector
        def _(e):
            pl.replay("vector", e, sems)

        @block.scalar
        def _(e):
            pl.replay("scalar", e, sems)

    return nc


def kernel(L0_attn_x, L0_attn_Wq, L0_attn_Wk, L0_attn_Wv, L0_attn_Wf,
           L0_attn_g_norm_w, L0_attn_Wo, L0_ffn_W1, L0_ffn_b1,
           L0_ffn_quantum_phases, L0_ffn_quantum_weights_real,
           L0_ffn_quantum_weights_imag, L0_ffn_W2, L0_ffn_b2):
    x = np.ascontiguousarray(np.asarray(L0_attn_x, dtype=np.float32))
    cm = np.ones((128, 128), np.float32)
    base = {
        "wq": np.ascontiguousarray(np.asarray(L0_attn_Wq, np.float32)),
        "wk": np.ascontiguousarray(np.asarray(L0_attn_Wk, np.float32)),
        "wv": np.ascontiguousarray(np.asarray(L0_attn_Wv, np.float32)),
        "wf": np.ascontiguousarray(np.asarray(L0_attn_Wf, np.float32)),
        "gnw": np.ascontiguousarray(np.asarray(L0_attn_g_norm_w, np.float32)),
        "wo": np.ascontiguousarray(np.asarray(L0_attn_Wo, np.float32)),
        "w1": np.ascontiguousarray(np.asarray(L0_ffn_W1, np.float32)),
        "b1": np.ascontiguousarray(np.asarray(L0_ffn_b1, np.float32)),
        "qph": np.ascontiguousarray(np.asarray(L0_ffn_quantum_phases, np.float32)),
        "qwr": np.ascontiguousarray(np.asarray(L0_ffn_quantum_weights_real, np.float32)),
        "qwi": np.ascontiguousarray(np.asarray(L0_ffn_quantum_weights_imag, np.float32)),
        "w2": np.ascontiguousarray(np.asarray(L0_ffn_W2, np.float32)),
        "b2": np.ascontiguousarray(np.asarray(L0_ffn_b2, np.float32)),
        "ctri": np.triu(cm).copy(),
        "cidb": np.eye(128, dtype=np.float32).astype(BFNP),
        "cidf": np.eye(128, dtype=np.float32),
        "cones": cm,
        "cones1b": np.ones((1, 512), np.float32).astype(BFNP),
        "cbias": np.broadcast_to(np.array([LNS, 1.0, EPS, PI / 2, MAGICR, 0, 0, 0],
                                          np.float32), (128, 8)).copy(),
    }
    nc = build_program(debug=False)
    in_maps = []
    for i in range(NCORES):
        m = dict(base)
        m["x"] = np.ascontiguousarray(x[i * SEQ:(i + 1) * SEQ])
        in_maps.append(m)
    res = run_bass_kernel_spmd(nc, in_maps, core_ids=list(range(NCORES)))
    return np.concatenate([r["y"] for r in res.results], axis=0)


# version 24
# speedup vs baseline: 1.0321x; 1.0321x over previous
import numpy as np
import ml_dtypes
import concourse.bass as bass
import concourse.mybir as mybir
from concourse.bass_utils import run_bass_kernel_spmd

F32 = mybir.dt.float32
BF16 = mybir.dt.bfloat16
I32 = mybir.dt.int32
AF = mybir.ActivationFunctionType
ALU = mybir.AluOpType
AX = mybir.AxisListType
BFNP = ml_dtypes.bfloat16

B, T, D = 16, 2048, 1024
H, KD, VD, MD, DFF = 4, 256, 256, 64, 4096
NCORES = 8
SEQ = B // NCORES
NT = SEQ * T
CA = 512
NCHA = NT // CA
CHPS = T // CA
CB = 256
NCHB = NT // CB
GLN = 8.0
EPS = 1e-5
LNS = float(np.log(KD ** -0.5))
PI = float(np.pi)
MAGIC = 12582912.0
MAGICR = float(np.frombuffer(np.uint32(0x5F3759DF).tobytes(), np.float32)[0])
STATE_EN = True
STATE_FORCE = False
TE_SELFWAIT_DROP = True
CHAIN_LAG = 2
DROP_STAGES = {"st2", "st3", "ktm", "st4", "st5", "st8", "pass1", "pass2w", "pass2o", "state"}

ENGINES = ("sync", "gpsimd", "tensor", "vector", "scalar")
DMA_ENG = {"sync", "gpsimd"}
SEM_POOL = {"sync": ["sy%d" % i for i in range(8)],
            "gpsimd": ["gp%d" % i for i in range(4)],
            "tensor": ["te"], "vector": ["ve"], "scalar": ["sc"],
            "pool": ["gpc"]}
ALL_SEMS = [s for v in SEM_POOL.values() for s in v]


class Plan:
    def __init__(self):
        self.ops = {e: [] for e in ENGINES}
        self.nop = {e: 0 for e in list(ENGINES) + ["pool"]}
        self.cnt = {s: 0 for s in ALL_SEMS}
        self.pending = {e: None for e in ENGINES}
        self.drop_te = False

    def op(self, eng, build, wait=()):
        stream = "gpsimd" if eng == "pool" else eng
        w = []
        if self.pending[stream] is not None:
            w.extend(self.pending[stream])
            self.pending[stream] = None
        pool = SEM_POOL[eng]
        if eng in DMA_ENG:
            sem = pool[self.nop[eng] % len(pool)]
            if self.cnt[sem] > 0:
                w.append((sem, self.cnt[sem]))
            step = 16
        else:
            sem = pool[0]
            step = 1
            if eng in ("vector", "scalar", "pool") and self.cnt[sem] > 0:
                w.append((sem, max(1, self.cnt[sem] - (CHAIN_LAG - 1))))
        flat = []

        def _flat(t):
            if t is None:
                return
            if isinstance(t, list):
                for x in t:
                    _flat(x)
            else:
                flat.append(t)
        for t in wait:
            _flat(t)
        for se, sv in flat:
            if self.drop_te and eng == "tensor" and se == "te":
                continue
            w.append((se, sv))
        self.nop[eng] += 1
        self.cnt[sem] += step
        self.ops[stream].append((build, tuple(w), sem, step))
        return (sem, self.cnt[sem])

    def snapshot(self):
        return [(s, self.cnt[s]) for s in ALL_SEMS if self.cnt[s] > 0]

    def set_pending(self, snap):
        for e in ENGINES:
            self.pending[e] = list(snap)

    def barrier(self):
        self.set_pending(self.snapshot())

    def replay(self, eng, proxy, sems):
        hw = {}
        for build, waits, sem, step in self.ops[eng]:
            for (se, sv) in waits:
                if hw.get(se, -1) < sv:
                    proxy.wait_ge(sems[se], sv)
                    hw[se] = sv
            build(proxy).then_inc(sems[sem], step)

    def final_wait(self, eng, proxy, sems):
        for s in ALL_SEMS:
            if self.cnt[s] > 0:
                proxy.wait_ge(sems[s], self.cnt[s])


def build_program(debug=False):
    nc = bass.Bass("TRN2", target_bir_lowering=False)

    x_h = nc.dram_tensor("x", [SEQ, T, D], F32, kind="ExternalInput")
    wq_h = nc.dram_tensor("wq", [D, D], F32, kind="ExternalInput")
    wk_h = nc.dram_tensor("wk", [D, D], F32, kind="ExternalInput")
    wv_h = nc.dram_tensor("wv", [D, D], F32, kind="ExternalInput")
    wf_h = nc.dram_tensor("wf", [D, H * MD], F32, kind="ExternalInput")
    gnw_h = nc.dram_tensor("gnw", [VD], F32, kind="ExternalInput")
    wo_h = nc.dram_tensor("wo", [D, D], F32, kind="ExternalInput")
    w1_h = nc.dram_tensor("w1", [D, DFF], F32, kind="ExternalInput")
    b1_h = nc.dram_tensor("b1", [DFF], F32, kind="ExternalInput")
    qph_h = nc.dram_tensor("qph", [DFF], F32, kind="ExternalInput")
    qwr_h = nc.dram_tensor("qwr", [DFF], F32, kind="ExternalInput")
    qwi_h = nc.dram_tensor("qwi", [DFF], F32, kind="ExternalInput")
    w2_h = nc.dram_tensor("w2", [DFF, D], F32, kind="ExternalInput")
    b2_h = nc.dram_tensor("b2", [D], F32, kind="ExternalInput")
    ctri_h = nc.dram_tensor("ctri", [128, 128], F32, kind="ExternalInput")
    cidb_h = nc.dram_tensor("cidb", [128, 128], BF16, kind="ExternalInput")
    cidf_h = nc.dram_tensor("cidf", [128, 128], F32, kind="ExternalInput")
    cones_h = nc.dram_tensor("cones", [128, 128], F32, kind="ExternalInput")
    co1b_h = nc.dram_tensor("cones1b", [1, 512], BF16, kind="ExternalInput")
    cbias_h = nc.dram_tensor("cbias", [128, 8], F32, kind="ExternalInput")
    y_h = nc.dram_tensor("y", [SEQ, T, D], F32, kind="ExternalOutput")
    x1tm_h = nc.dram_tensor("x1tm", [NT, D], F32, kind="Internal")
    x1fm_h = nc.dram_tensor("x1fm", [D, NT], BF16, kind="Internal")
    ebc_h = nc.dram_tensor("ebcscr", [256], F32, kind="Internal")

    dbg = {}
    if debug:
        for nm, shp in [("dq", [128, 4096]), ("dk", [128, 4096]),
                        ("dv", [128, 4096]), ("df", [128, 1024]),
                        ("ds", [128, 1024]), ("deb", [128, 1024]),
                        ("dotm", [128, 4096]), ("dx1", [128, 1024]),
                        ("dhid", [128, 256]), ("dktm", [128, 4096])]:
            dbg[nm] = nc.dram_tensor(nm, shp, F32, kind="ExternalOutput")
        dbg["dP"] = nc.dram_tensor("dP", [128, 1024], BF16, kind="ExternalOutput")
        dbg["dact"] = nc.dram_tensor("dact", [128, 8192], BF16, kind="ExternalOutput")

    AW = nc.alloc_sbuf_tensor("AW", [128, 34816], BF16)
    off = {}
    cur = [0]

    def al(name, elems):
        cur[0] = (cur[0] + 15) // 16 * 16
        off[name] = cur[0]
        cur[0] += elems

    for nm, n in [("q_fm", 4096), ("k_fm", 4096), ("k_tm", 4096),
                  ("v_tm", 4096), ("o_tm", 4096),
                  ("f", 2048), ("eb", 2048), ("ebs", 2048),
                  ("enb", 2048), ("P", 1024), ("tmpa", 1024), ("tmpb", 1024),
                  ("o_fmt", 2048), ("x1pt", 2048), ("x1bt", 1024),
                  ("x1fmt", 1024), ("x1tmt", 2048), ("pt2", 512),
                  ("pt2T", 1024), ("PT", 1024), ("spm", 256), ("wtm", 256),
                  ("lbuf", 512), ("e1b", 512), ("bcb", 128), ("ebcr", 512),
                  ("ebcc", 16), ("hk0", 1024), ("hk1", 1024), ("hkb", 512),
                  ("hv0", 2048), ("hv1", 2048), ("hvb", 1024)]:
        al(nm, n)
    zone_a_end = max(cur[0], 50192)
    off["W2"], off["ACT"] = 0, 32768
    off["b1row"], off["b2row"] = 40960, 45056
    off["ypt"], off["yt"] = 46080, 48128
    cur[0] = zone_a_end
    for nm, n in [("x_tm", 8192), ("xb", 4096), ("x_fm", 4096), ("cols", 128),
                  ("b1c", 192),
                  ("gcols", 16), ("d1c", 64), ("d2c", 64), ("ctri", 256),
                  ("cidb", 128), ("cidf", 256), ("cones", 256), ("co1b", 512),
                  ("cbias", 16)]:
        al(nm, n)
    AC = nc.alloc_sbuf_tensor("AC", [128, (cur[0] + 15) // 16 * 16], BF16)
    ACF = AC.bitcast(F32)
    ACI = AC.bitcast(I32)

    def coli(i):
        o = off["cols"] // 2 + i
        return ACI[0:128, o:o + 1]

    def emit_rsqrt2(OP, dsti, srci, scri, n, wait=()):
        ACIc = lambda i, n: ACI[0:128, off["cols"] // 2 + i:
                                off["cols"] // 2 + i + n]
        MGI = ACI[0:128, off["cbias"] // 2 + 4: off["cbias"] // 2 + 5]
        tk = OP("vector", lambda v: v.tensor_scalar(
            ACIc(dsti, n), ACIc(srci, n), 1, None, ALU.logical_shift_right),
            wait=list(wait))
        tk = OP("vector", lambda v: v.tensor_tensor(
            ACIc(dsti, n), MGI.broadcast_to([128, n]) if n > 1 else MGI,
            ACIc(dsti, n), ALU.subtract), wait=[tk])
        for _ in range(2):
            tk = OP("vector", lambda v: v.tensor_mul(
                vf("cols", scri, n), vf("cols", dsti, n), vf("cols", dsti, n)),
                wait=[tk])
            tk = OP("vector", lambda v: v.tensor_mul(
                vf("cols", scri, n), vf("cols", scri, n), vf("cols", srci, n)),
                wait=[tk])
            tk = OP("vector", lambda v: v.tensor_scalar(
                vf("cols", scri, n), vf("cols", scri, n), -0.5, 1.5,
                ALU.mult, ALU.add), wait=[tk])
            tk = OP("vector", lambda v: v.tensor_mul(
                vf("cols", dsti, n), vf("cols", dsti, n), vf("cols", scri, n)),
                wait=[tk])
        return tk

    def vb(name, c0, n, p0=0, p1=128):
        o = off[name] + c0
        return AC[p0:p1, o:o + n]

    def vf(name, c0, n, p0=0, p1=128):
        o = off[name] // 2 + c0
        return ACF[p0:p1, o:o + n]

    x_tm = lambda c0, n: vf("x_tm", c0, n)
    xb_v = lambda c0, n: vb("xb", c0, n)
    tmpc = lambda c0, n: vf("xb", 1024 + c0, n)
    x_fm = lambda c0, n: vb("x_fm", c0, n)
    q_fm = lambda c0, n: vb("q_fm", c0, n)
    k_fm = lambda c0, n: vb("k_fm", c0, n)
    k_tm = lambda c0, n: vb("k_tm", c0, n)
    v_tm = lambda c0, n: vb("v_tm", c0, n)
    o_tmv = lambda c0, n: vb("o_tm", c0, n)
    f_v = lambda c0, n: vf("f", c0, n)
    eb_v = lambda c0, n: vf("eb", c0, n)
    ebs_v = lambda c0, n: vf("ebs", c0, n)
    enb_v = lambda c0, n: vf("enb", c0, n)
    P_v = lambda c0, n: vb("P", c0, n)
    tmpa = lambda c0, n: vf("tmpa", c0, n)
    tmpb = lambda c0, n: vf("tmpb", c0, n)
    col = lambda i: vf("cols", i, 1)
    HKg = lambda g, c0, n: vf("hk%d" % g, c0, n)
    HVg = lambda g, c0, n: vf("hv%d" % g, c0, n, 0, 64)
    HKB = lambda c0, n: vb("hkb", c0, n)
    HVB = lambda c0, n, p0=0: vb("hvb", c0, n, p0, p0 + 64)
    SPM = lambda sl: vb("spm", sl * 128, 128)
    WTMv = lambda sl: vb("wtm", sl * 128, 128)
    BCB = vf("bcb", 0, 64)
    EBCR = vf("ebcr", 0, 256, 0, 1)
    EBCC = vf("ebcc", 0, 4, 0, 64)

    PS = [nc.alloc_psum_tensor("ps%d" % i, [128, 512], F32) for i in range(7)]
    P7 = nc.alloc_psum_tensor("ps7", [128, 1024], BF16)
    PS0B = PS[0].bitcast(BF16)
    PS5B = PS[5].bitcast(BF16)
    PS1B = PS[1].bitcast(BF16)
    PS6B = PS[6].bitcast(BF16)

    nc._dbg_off = off
    pl = Plan()
    OP = pl.op

    def MM(out, lhsT, rhs, start, stop, wait=()):
        return OP("tensor", lambda t, o=out, l=lhsT, r=rhs, a=start, b=stop:
                  t.matmul(o, l, r, start=a, stop=b), wait=wait)

    def TRG(out, in_, ident, start, stop, wait=()):
        return OP("tensor", lambda t, o=out, i=in_, d=ident, a=start, b=stop:
                  t.matmul(o, i, d, start=a, stop=b, is_transpose=True),
                  wait=wait)

    tk_tri = OP("sync", lambda e: e.dma_start(out=vf("ctri", 0, 128), in_=ctri_h.ap()))
    tk_idb = OP("sync", lambda e: e.dma_start(out=vb("cidb", 0, 128), in_=cidb_h.ap()))
    tk_idf = OP("sync", lambda e: e.dma_start(out=vf("cidf", 0, 128), in_=cidf_h.ap()))
    tk_ones = OP("sync", lambda e: e.dma_start(out=vf("cones", 0, 128), in_=cones_h.ap()))
    tk_o1b = OP("sync", lambda e: e.dma_start(out=vb("co1b", 0, 512, 0, 1), in_=co1b_h.ap()))
    OP("sync", lambda e: e.dma_start(out=vf("cbias", 0, 8), in_=cbias_h.ap()))
    for _i, _val in enumerate([LNS, 1.0, EPS, PI / 2, MAGICR]):
        nc.const_aps.register(F32, float(_val), vf("cbias", _i, 1))
    TRI = vf("ctri", 0, 128)
    IDB = vb("cidb", 0, 128)
    IDF = vf("cidf", 0, 128)
    ONES = vf("cones", 0, 128)
    ONES1B = vb("co1b", 0, 512, 0, 1)

    OP("gpsimd", lambda e: e.dma_start(
        out=AW[:, 0:8192].rearrange("p (dc f) -> p dc f", dc=8), in_=wq_h.ap().rearrange("(dc p) f -> p dc f", p=128)))
    OP("gpsimd", lambda e: e.dma_start(
        out=AW[:, 8192:16384].rearrange("p (dc f) -> p dc f", dc=8), in_=wk_h.ap().rearrange("(dc p) f -> p dc f", p=128)))
    OP("gpsimd", lambda e: e.dma_start(
        out=AW[:, 16384:24576].rearrange("p (dc f) -> p dc f", dc=8), in_=wv_h.ap().rearrange("(dc p) f -> p dc f", p=128)))
    tk_wf = OP("gpsimd", lambda e: e.dma_start(
        out=AW[:, 32768:34816].rearrange("p (dc f) -> p dc f", dc=8), in_=wf_h.ap().rearrange("(dc p) f -> p dc f", p=128)))
    WQ = lambda c0, n: AW[:, c0:c0 + n]
    WK = lambda c0, n: AW[:, 8192 + c0:8192 + c0 + n]
    WV = lambda c0, n: AW[:, 16384 + c0:16384 + c0 + n]
    WOv = lambda c0, n: AW[:, 24576 + c0:24576 + c0 + n]
    WFv = lambda c0, n: AW[:, 32768 + c0:32768 + c0 + n]

    tk_g0 = OP("sync", lambda e: e.dma_start(
        out=vf("gcols", 0, 1), in_=gnw_h.ap().rearrange("(c p) -> p c", p=128)[:, 0:1]))
    tk_g1 = OP("sync", lambda e: e.dma_start(
        out=vf("gcols", 1, 1), in_=gnw_h.ap().rearrange("(c p) -> p c", p=128)[:, 1:2]))

    stg = lambda i: vf("o_tm", i * 1024, 1024)
    stg_free = [None, None]
    for hvc in range(8):
        sl = hvc % 2
        tkl = OP("sync", lambda e, hvc=hvc, sl=sl: e.dma_start(
            out=stg(sl), in_=wo_h.ap()[hvc * 128:(hvc + 1) * 128, :]),
            wait=[stg_free[sl]])
        stg_free[sl] = OP("vector", lambda v, hvc=hvc, sl=sl: v.tensor_scalar_mul(
            WOv(hvc * 1024, 1024), stg(sl), vf("gcols", hvc % 2, 1)),
            wait=[tkl, tk_g0, tk_g1])

    def trow(i):
        return ACF[0:32, off["xb"] // 2 + i * 128: off["xb"] // 2 + (i + 1) * 128]

    def trowi(i):
        return ACI[0:32, off["xb"] // 2 + i * 128: off["xb"] // 2 + (i + 1) * 128]
    tk_th = OP("sync", lambda e: e.dma_start(out=trow(0), in_=qph_h.ap().rearrange("(r p) -> r p", p=128)))
    tk_wr = OP("sync", lambda e: e.dma_start(out=trow(1), in_=qwr_h.ap().rearrange("(r p) -> r p", p=128)))
    tk_wi = OP("sync", lambda e: e.dma_start(out=trow(2), in_=qwi_h.ap().rearrange("(r p) -> r p", p=128)))
    t_ = OP("vector", lambda v: v.tensor_scalar(trow(3), trow(0), 1.0 / (2 * PI), MAGIC, ALU.mult, ALU.add), wait=[tk_th])
    t_ = OP("vector", lambda v: v.tensor_copy(trowi(3), trowi(3)), wait=[t_])
    t_ = OP("vector", lambda v: v.tensor_scalar_add(trow(3), trow(3), -MAGIC), wait=[t_])
    tra = OP("vector", lambda v: v.scalar_tensor_tensor(trow(3), trow(3), -2 * PI, trow(0), ALU.mult, ALU.add), wait=[t_])
    t_ = OP("vector", lambda v: v.tensor_scalar_add(trow(5), trow(0), PI / 2), wait=[tk_th])
    t_ = OP("vector", lambda v: v.tensor_scalar(trow(6), trow(5), 1.0 / (2 * PI), MAGIC, ALU.mult, ALU.add), wait=[t_])
    t_ = OP("vector", lambda v: v.tensor_copy(trowi(6), trowi(6)), wait=[t_])
    t_ = OP("vector", lambda v: v.tensor_scalar_add(trow(6), trow(6), -MAGIC), wait=[t_])
    trb = OP("vector", lambda v: v.scalar_tensor_tensor(trow(6), trow(6), -2 * PI, trow(5), ALU.mult, ALU.add), wait=[t_])
    ts1 = OP("scalar", lambda s: s.activation(trow(4), trow(3), AF.Sin), wait=[tra])
    ts2 = OP("scalar", lambda s: s.activation(trow(5), trow(6), AF.Sin), wait=[trb])
    t_ = OP("vector", lambda v: v.tensor_mul(trow(6), trow(5), trow(1)), wait=[ts2, tk_wr, trb])
    t2_ = OP("vector", lambda v: v.tensor_mul(trow(7), trow(4), trow(2)), wait=[ts1, tk_wi, t_])
    t_ = OP("vector", lambda v: v.tensor_add(trow(6), trow(6), trow(7)), wait=[t2_])
    t_ = OP("vector", lambda v: v.tensor_mul(trow(6), trow(6), trow(2)), wait=[t_])
    t_ = OP("vector", lambda v: v.tensor_scalar_mul(trow(6), trow(6), 0.1), wait=[t_])
    t2_ = OP("vector", lambda v: v.tensor_mul(trow(7), trow(6), trow(4)), wait=[t_])
    td = OP("vector", lambda v: v.tensor_mul(trow(6), trow(6), trow(5)), wait=[t2_])
    tt1 = TRG(PS[0][0:128, 0:32], trow(7), IDF[0:32, 0:32], True, True, wait=[td, tk_idf])
    tt2 = TRG(PS[1][0:128, 0:32], trow(6), IDF[0:32, 0:32], True, True, wait=[td])
    te_d1 = OP("vector", lambda v: v.tensor_copy(vf("d1c", 0, 32), PS[0][0:128, 0:32]), wait=[tt1])
    te_d2 = OP("vector", lambda v: v.tensor_copy(vf("d2c", 0, 32), PS[1][0:128, 0:32]), wait=[tt2])

    pl.barrier()

    pfree = {"P7": None, "PS0B": None, "PS1B": None}
    ps1_free = None
    abank_free = {0: None, 1: None}
    o_free = [None, None]
    prev_sq = prev_var = prev_x1 = prev_st1 = prev_st2 = None
    prev_pe2 = None
    prev_e1 = prev_p2a = None
    prev_softmax_done = None
    prev_state_done = []
    prev_ebc_readers = []
    prev_ebcr_dma = None
    prev_pe_tail = None
    prev_rms = None
    preload_x = []

    def sdrop(name):
        pl.drop_te = TE_SELFWAIT_DROP and (name in DROP_STAGES)

    fronts = {}
    xbpre = {}

    def emit_front(cc, ps56_free, lget):
        csq, ct0 = cc // CHPS, (cc % CHPS) * CA
        if cc not in xbpre:
            xbpre[cc] = OP("gpsimd", lambda e: e.dma_start(
                out=xb_v(0, 4096).rearrange("p (tt d) -> p tt d", tt=4),
                in_=x_h.ap()[csq, ct0:ct0 + CA, :].rearrange(
                    "(tt p) d -> p tt d", p=128)))
        sdrop("st2")
        tkb = {0: ps56_free, 1: ps56_free}
        for r in range(4):
            tb_ = r % 2
            TPB = PS5B if tb_ == 0 else PS6B
            tk_pe = None
            for dc in range(8):
                tk_pe = TRG(TPB[:, dc * 128:(dc + 1) * 128],
                            xb_v(r * 1024 + dc * 128, 128), IDB, dc == 0, dc == 7,
                            wait=[xbpre[cc], tk_idb, tkb[tb_], tk_pe])
            tkb[tb_] = OP("vector", lambda v, r=r, TPB=TPB: v.tensor_copy(
                AC[:, off["x_fm"]:off["x_fm"] + 4096].rearrange(
                    "p (dc t) -> p dc t", dc=8)[:, :, r * 128:(r + 1) * 128],
                TPB[:, :].rearrange("p (dc t) -> p dc t", dc=8)), wait=[tk_pe])
        xfm = tkb[1]
        if cc + 1 < NCHA:
            nsq_, nt0_ = (cc + 1) // CHPS, ((cc + 1) % CHPS) * CA
            xbpre[cc + 1] = OP("gpsimd", lambda e, nsq_=nsq_, nt0_=nt0_:
                               e.dma_start(
                out=xb_v(0, 4096).rearrange("p (tt d) -> p tt d", tt=4),
                in_=x_h.ap()[nsq_, nt0_:nt0_ + CA, :].rearrange(
                    "(tt p) d -> p tt d", p=128)), wait=[xfm])
        st = {"xfm": xfm, "tb1": tkb[1], "tb0": tkb[0], "i": 0,
              "qkf": None, "q_all": None, "k_all": None}

        def pump(n):
            sdrop("st3")
            if st["qkf"] is None:
                lf = lget()
                st["qkf"] = {0: [lf[0], lf[1], tkb[0]],
                             1: [lf[0], lf[1], xfm]}
            while n > 0 and st["i"] < 16:
                n -= 1
                i = st["i"]
                st["i"] += 1
                W, dst = (WQ, q_fm) if i < 8 else (WK, k_fm)
                qc = i % 8
                sl = qc % 2
                ps = PS[5] if sl == 0 else PS[6]
                tk_mm = None
                for dc in range(8):
                    tk_mm = MM(ps[:, 0:CA], W(dc * 1024 + qc * 128, 128),
                               x_fm(dc * 512, 512), dc == 0, dc == 7,
                               wait=[xfm, st["qkf"][sl]])
                ev = OP("scalar", lambda s_, ps=ps, dst=dst, qc=qc:
                        s_.activation(dst(qc * 512, 512), ps[:, 0:CA],
                                      AF.Silu),
                        wait=[tk_mm])
                st["qkf"][sl] = ev
                if i == 7:
                    st["q_all"] = [ev, st["qkf"][0]]
                if i == 15:
                    st["k_all"] = [ev, st["qkf"][0]]
        st["pump"] = pump
        return st
    for c in range(NCHA):
        sq, lc = c // CHPS, c % CHPS
        first = (lc == 0) or not STATE_EN
        t0 = lc * CA
        g_st, g_ld = c % 2, (c + 1) % 2

        if c == 0:
            preload_x = []
            for tt in range(4):
                preload_x.append(OP("sync", lambda e, tt=tt, sq=sq, t0=t0: e.dma_start(
                    out=x_tm(tt * 1024, 1024),
                    in_=x_h.ap()[sq, t0 + tt * 128: t0 + (tt + 1) * 128, :])))
        tk_x = preload_x

        if c == 0:
            fronts[0] = emit_front(0, None, lambda: [None, None])
            fronts[0]["pump"](16)
        fr = fronts.pop(c)
        tk_xfm = fr["xfm"]
        tk_q_all = fr["q_all"]
        tk_k_all = fr["k_all"]
        tbank_free = {0: pfree["P7"], 1: fr["tb1"]}
        sdrop("ktm")
        for tt in range(4):
            tb_ = tt % 2
            TPB = P7 if tb_ == 0 else PS6B
            tk_pe = None
            for kcg in range(8):
                tk_pe = TRG(TPB[:, kcg * 128:(kcg + 1) * 128],
                            k_fm(kcg * 512 + tt * 128, 128), IDB, kcg == 0, kcg == 7,
                            wait=[tk_k_all, tbank_free[tb_], tk_pe])
            tbank_free[tb_] = OP("vector", lambda v, tt=tt, TPB=TPB: v.tensor_copy(
                k_tm(tt * 1024, 1024), TPB[:, :]), wait=[tk_pe])
        tk_ktm = tbank_free[1]
        tk_ktm0 = tbank_free[0]

        sdrop("st4")
        tk_vv, tk_ff, tk_ss = {}, {}, {}
        tk_sgs = {}
        tk_vev = None
        fp_free = [[abank_free[1]], [abank_free[1]]]
        prev_tkss = None
        for tt in range(4):
            psl = tt % 2
            pf = (PS[2] if psl == 0 else PS[3])[:, 0:256]
            tk_mmv = tk_mmf = None
            for dc in range(8):
                xt = x_fm(dc * 512 + tt * 128, 128)
                for half in range(2):
                    tk_mmv = MM(PS[5 + half][:, 0:512], xt,
                                WV(dc * 1024 + half * 512, 512), dc == 0, dc == 7,
                                wait=[tk_xfm, tk_vev, tk_mmv, tk_ktm])
                tk_mmf = MM(pf, xt, WFv(dc * 256, 256), dc == 0, dc == 7,
                            wait=[tk_wf, tk_mmf] + fp_free[psl])
            OP("vector", lambda v, tt=tt: v.tensor_copy(
                v_tm(tt * 1024, 512), PS[5][:, 0:512]), wait=[tk_mmv])
            tk_vev = OP("vector", lambda v, tt=tt: v.tensor_copy(
                v_tm(tt * 1024 + 512, 512), PS[6][:, 0:512]), wait=[tk_mmv])
            tk_vv[tt] = tk_vev
            tk_sg = OP("scalar", lambda s, pf=pf, tt=tt: s.activation(
                f_v(tt * 256, 256), pf, AF.Sigmoid),
                wait=[tk_mmf, prev_tkss, prev_pe_tail])
            tk_sgs[tt] = tk_sg
            fp_free[psl] = [tk_sg]
            prev_tkss = tk_sg

        for tt in range(4):
            tk_ff[tt] = OP("scalar", lambda s, tt=tt: s.activation(
                f_v(tt * 256, 256), f_v(tt * 256, 256), AF.Ln),
                wait=[tk_sgs[tt]])

        sdrop("st5")
        tk_eb, tk_ebs, tk_enb, tk_P = {}, {}, {}, {}
        for tt in range(4):
            bs = (PS[2] if tt % 2 == 0 else PS[3])[:, 0:256]
            tk_mm = None
            for r in range(tt):
                tk_mm = MM(bs, ONES, f_v(r * 256, 256), r == 0, False,
                           wait=[tk_ff[r], tk_ones, tk_mm,
                                 tk_enb.get(tt - 2)] + fp_free[tt % 2])
            tk_bmm = MM(bs, TRI, f_v(tt * 256, 256), tt == 0, True,
                        wait=[tk_ff[tt], tk_tri, tk_mm,
                              tk_enb.get(tt - 2)] + fp_free[tt % 2])
            tk_eb[tt] = OP("scalar", lambda s, tt=tt, bs=bs: s.activation(
                eb_v(tt * 256, 256), bs, AF.Exp, scale=1.0 / GLN),
                wait=[tk_bmm, prev_ebcr_dma])
            tk_ebs[tt] = OP("scalar", lambda s, tt=tt, bs=bs: s.activation(
                ebs_v(tt * 256, 256), bs, AF.Exp, scale=1.0 / GLN,
                bias=LNS), wait=[tk_bmm])
            tk_enb[tt] = OP("scalar", lambda s, tt=tt, bs=bs: s.activation(
                enb_v(tt * 256, 256), bs, AF.Exp, scale=-1.0 / GLN),
                wait=[tk_bmm])
            t_ef = OP("scalar", lambda s, tt=tt: s.activation(
                vf("lbuf", 0, 256), f_v(tt * 256, 256), AF.Exp,
                scale=1.0 / GLN),
                wait=[tk_bmm, tk_P.get(tt - 1), prev_softmax_done])
            t_s = OP("vector", lambda v: v.tensor_scalar(
                vf("e1b", 0, 256), vf("lbuf", 0, 256), -1.0, 1.0,
                ALU.mult, ALU.add), wait=[t_ef])
            tk_P[tt] = OP("vector", lambda v, tt=tt: v.tensor_mul(
                P_v(tt * 256, 256), vf("e1b", 0, 256), enb_v(tt * 256, 256)),
                wait=[tk_enb[tt], t_s, prev_pe_tail])
        tk_ebc0 = OP("sync", lambda e: e.dma_start(
            out=ebc_h.ap(), in_=vf("eb", 768, 256, 127, 128)), wait=[tk_eb[3]])
        tk_ebcr = OP("sync", lambda e: e.dma_start(
            out=EBCR, in_=ebc_h.ap().rearrange("(a n) -> a n", a=1)),
            wait=[tk_ebc0])
        prev_ebcr_dma = tk_ebc0
        tk_ebcc = OP("gpsimd", lambda e: e.dma_start(
            out=EBCC, in_=ebc_h.ap().rearrange("(h m) -> m h", m=64),
            allow_slow_non_contiguous=True), wait=[tk_ebc0])

        sp_free = [pfree["PS0B"], ps1_free]
        l_free = [tk_enb[2], tk_enb[3]]
        wt_free = [None, None]
        dk_free = None
        dv_free = None
        tk_state_tail = []
        tk_o_ev = {}
        ssl_ctr = [0]
        wsl_ctr = [0]
        LBUF = vf("lbuf", 0, 256)
        E1F = vf("e1b", 0, 256)
        PT2A = lambda sl: vb("pt2", sl * 256, 256)
        tk_pt_ev = {}
        tk_pt2w = {0: None, 1: None}
        tk_softmax = {}
        sdrop("pass1")

        def emit_tp(tt, tk_p2):
            nonlocal_state = None
            tk_tp = None
            prev_ev = tk_pt_ev.get(tt - 1)
            sl = tt % 2
            for i in range(2):
                tk_tp = TRG(P7[:, i * 128:(i + 1) * 128],
                            PT2A(sl)[:, i * 128:(i + 1) * 128],
                            IDB, i == 0, False,
                            wait=[tk_p2, prev_ev, tk_ktm0, tk_tp])
            for i in range(2):
                tk_tp = TRG(P7[:, 256 + i * 128:256 + (i + 1) * 128],
                            P_v(tt * 256 + i * 128, 128), IDB, False, i == 1,
                            wait=[tk_P[tt], tk_tp])
            tk_pt2w[sl] = tk_tp
            tk_ev = OP("vector", lambda v, tt=tt: v.tensor_copy(
                vb("pt2T", 0, 1024).rearrange("p (pr t) -> p pr t", pr=2)[
                    :, :, tt * 128:(tt + 1) * 128],
                P7[:, 0:256].rearrange("p (pr t) -> p pr t", pr=2)), wait=[tk_tp])
            tk_ev = OP("vector", lambda v, tt=tt: v.tensor_copy(
                vb("PT", 0, 1024).rearrange("p (pr t) -> p pr t", pr=2)[
                    :, :, tt * 128:(tt + 1) * 128],
                P7[:, 256:512].rearrange("p (pr t) -> p pr t", pr=2)),
                wait=[tk_tp, tk_ev])
            tk_pt_ev[tt] = tk_ev

        tp_pend = None
        for tt in range(4):
            lsl = tt % 2
            L = (PS[2] if lsl == 0 else PS[3])[:, 0:256]
            tk_lmm = None
            started = [False]

            def lflag():
                st = not started[0]
                started[0] = True
                return st
            if not first:
                for h in range(4):
                    for kc in range(2):
                        tk_lmm = MM(L[:, h * 64:h * 64 + 64],
                                    q_fm((2 * h + kc) * 512 + tt * 128, 128),
                                    HKB((2 * h + kc) * 64, 64), lflag(), False,
                                    wait=[tk_q_all, l_free[lsl], tk_lmm] + prev_state_done)
            for h in range(4):
                for jt in range(tt + 1):
                    ssl = ssl_ctr[0] % 2
                    ssl_ctr[0] += 1
                    S = (PS[0] if ssl == 0 else PS[1])[:, 0:128]
                    tk_smm = MM(S, k_fm((2 * h) * 512 + jt * 128, 128),
                                q_fm((2 * h) * 512 + tt * 128, 128),
                                True, False,
                                wait=[tk_q_all, tk_k_all, sp_free[ssl]])
                    tk_smm = MM(S, k_fm((2 * h + 1) * 512 + jt * 128, 128),
                                q_fm((2 * h + 1) * 512 + tt * 128, 128),
                                False, True, wait=[tk_smm])
                    if jt == tt:
                        tk_spm = OP("vector", lambda v, S=S, ssl=ssl: v.tensor_mul(
                            SPM(ssl), S, TRI), wait=[tk_smm])
                    else:
                        tk_spm = OP("scalar", lambda s_, S=S, ssl=ssl: s_.activation(
                            SPM(ssl), S, AF.Copy), wait=[tk_smm])
                    sp_free[ssl] = tk_spm
                    tk_lmm = MM(L[:, h * 64:h * 64 + 64], SPM(ssl),
                                P_v(jt * 256 + h * 64, 64),
                                lflag(), (h == 3) and (jt == tt),
                                wait=[tk_spm, tk_P[jt], l_free[lsl], tk_lmm])
            if tp_pend is not None:
                emit_tp(*tp_pend)
            tk_lb = OP("vector", lambda v, L=L, tt=tt: v.tensor_mul(
                LBUF, L, ebs_v(tt * 256, 256)), wait=[tk_lmm, tk_ebs[tt]])
            l_free[lsl] = tk_lb
            tk_mx = OP("vector", lambda v: v.reduce_max(
                vf("cols", 0, 4), LBUF.rearrange("p (h m) -> p h m", h=4),
                axis=AX.X, negate=True), wait=[prev_e1, tk_lb])
            tk_sub = OP("vector", lambda v: v.tensor_add(
                E1F.rearrange("p (h m) -> p h m", h=4),
                LBUF.rearrange("p (h m) -> p h m", h=4),
                vf("cols", 0, 4).broadcast_to([128, 4, 64])), wait=[tk_mx])
            tk_e1 = OP("scalar", lambda s: s.activation(LBUF, E1F, AF.Exp),
                       wait=[tk_sub, prev_p2a])
            prev_e1 = tk_e1
            tk_sm = OP("vector", lambda v: v.reduce_sum(
                vf("cols", 4, 4), LBUF.rearrange("p (h m) -> p h m", h=4),
                axis=AX.X), wait=[tk_e1])
            tk_rc = OP("vector", lambda v: v.reciprocal(vf("cols", 8, 4),
                                                        vf("cols", 4, 4)),
                       wait=[tk_sm])
            tk_p2a = OP("vector", lambda v, tt=tt: v.tensor_mul(
                E1F, LBUF, eb_v(tt * 256, 256)), wait=[tk_e1, tk_eb[tt]])
            prev_p2a = tk_p2a
            tk_p2 = OP("vector", lambda v, tt=tt: v.tensor_mul(
                PT2A(tt % 2).rearrange("p (h m) -> p h m", h=4),
                E1F.rearrange("p (h m) -> p h m", h=4),
                vf("cols", 8, 4).broadcast_to([128, 4, 64])),
                wait=[tk_rc, tk_p2a, tk_pt2w[tt % 2]])
            tk_softmax[tt] = tk_p2
            prev_softmax_done = tk_p2
            tp_pend = (tt, tk_p2)
        emit_tp(*tp_pend)
        nxt_front = None
        if c + 1 < NCHA:
            nxt_front = emit_front(c + 1, tk_vv[3], lambda: l_free)
            fronts[c + 1] = nxt_front
        sdrop("pass2")
        obank_free = {0: [o_free[0]] + prev_state_done, 1: sp_free[0]}
        OBANK = {0: PS[4], 1: PS[0]}
        for tt in range(4):
            par = tt % 2
            for pair in range(2):
                OB = OBANK[pair][:, 0:512]
                tk_omm = None
                ostart = [True]

                def oflag():
                    st = ostart[0]
                    ostart[0] = False
                    return st
                if not first:
                    sdrop("pass2h")
                    for hh in range(2):
                        h = 2 * pair + hh
                        tk_omm = MM(OB[:, hh * 256:hh * 256 + 256],
                                    vb("pt2T", 0, 1024)[64 * hh:64 * hh + 64,
                                                        pair * 512 + tt * 128:
                                                        pair * 512 + tt * 128 + 128],
                                    HVB(h * 256, 256, 64 * hh), oflag(), False,
                                    wait=[tk_pt_ev[tt], obank_free[pair],
                                          tk_omm] + prev_state_done)
                for hh in range(2):
                    h = 2 * pair + hh
                    for jt in range(tt + 1):
                        wsl = wsl_ctr[0] % 2
                        wsl_ctr[0] += 1
                        Wp = (PS[2] if wsl == 0 else PS[3])[:, 0:128]
                        sdrop("pass2w")
                        tk_wmm = MM(Wp,
                                    vb("PT", 0, 1024)[64 * hh:64 * hh + 64,
                                                      pair * 512 + jt * 128:
                                                      pair * 512 + jt * 128 + 128],
                                    vb("pt2T", 0, 1024)[64 * hh:64 * hh + 64,
                                                        pair * 512 + tt * 128:
                                                        pair * 512 + tt * 128 + 128],
                                    True, True,
                                    wait=[tk_pt_ev[tt], wt_free[wsl],
                                          l_free[0], l_free[1]])
                        if jt == tt:
                            tk_wtm = OP("vector", lambda v, Wp=Wp, wsl=wsl: v.tensor_mul(
                                WTMv(wsl), Wp, TRI), wait=[tk_wmm])
                        else:
                            tk_wtm = OP("scalar", lambda s_, Wp=Wp, wsl=wsl: s_.activation(
                                WTMv(wsl), Wp, AF.Copy), wait=[tk_wmm])
                        wt_free[wsl] = tk_wtm
                        sdrop("pass2o")
                        tk_omm = MM(OB[:, hh * 256:hh * 256 + 256], WTMv(wsl),
                                    v_tm(jt * 1024 + h * 256, 256),
                                    oflag(), (hh == 1) and (jt == tt),
                                    wait=[tk_wtm, tk_vv[jt], tk_omm,
                                          obank_free[pair]
                                          if ostart[0] else None])
                tk_sq = OP("scalar", lambda s, OB=OB: s.activation(
                    tmpa(0, 512), OB, AF.Square), wait=[tk_omm, prev_rms])
                tk_ssq = OP("vector", lambda v: v.reduce_sum(
                    vf("cols", 12, 2), tmpa(0, 512).rearrange(
                        "p (g m) -> p g m", g=2), axis=AX.X), wait=[tk_sq])
                tk_ms = OP("vector", lambda v: v.tensor_scalar(
                    vf("cols", 14, 2), vf("cols", 12, 2), 1.0 / VD, EPS,
                    ALU.mult, ALU.add), wait=[tk_ssq])
                tk_rs = emit_rsqrt2(OP, 16, 14, 50, 2, wait=[tk_ms])
                tk_oe = None
                for hh in range(2):
                    h = 2 * pair + hh
                    tk_oe = OP("vector", lambda v, OB=OB, hh=hh, h=h, tt=tt:
                               v.tensor_scalar_mul(
                                   o_tmv(tt * 1024 + h * 256, 256),
                                   OB[:, hh * 256:hh * 256 + 256],
                                   vf("cols", 16 + hh, 1)), wait=[tk_rs, tk_oe])
                    tk_o_ev[(h, tt)] = tk_oe
                obank_free[pair] = tk_oe
                prev_rms = tk_oe
                if nxt_front is not None:
                    nxt_front["pump"](2)
        if nxt_front is not None:
            nxt_front["pump"](16)
        o_free = [obank_free[0], obank_free[1]]
        sp_free = [obank_free[1], sp_free[1]]
        pfree["PS0B"] = [pfree["PS0B"], obank_free[1]]
        sdrop("state")
        if STATE_EN and (lc < CHPS - 1 or STATE_FORCE):
            dkbf = sp_free[1]
            ps4f = obank_free[0]
            for h in range(4):
                tk_dk = {}
                tk_mm = None
                for kc in range(2):
                    DK = PS[1][:, kc * 64:kc * 64 + 64]
                    for jt in range(4):
                        tk_mm = MM(DK, k_tm(jt * 1024 + h * 256 + kc * 128, 128),
                                   P_v(jt * 256 + h * 64, 64),
                                   kc == 0 and jt == 0, kc == 1 and jt == 3,
                                   wait=[tk_ktm, tk_P[jt], dkbf,
                                         dk_free, tk_mm])
                    tk_dk[kc] = tk_mm
                tk_dk[0] = tk_dk[1]
                tk_bc = MM(PS[4][:, 0:64], ONES[0:1, :],
                           EBCR[0:1, h * 64: h * 64 + 64], True, True,
                           wait=[tk_ebcr, ps4f, dk_free])
                tk_bcb = OP("vector", lambda v: v.tensor_copy(BCB, PS[4][:, 0:64]),
                            wait=[tk_bc])
                tk_hkbw = None
                for kc in range(2):
                    DK = PS[1][:, kc * 64:kc * 64 + 64]
                    dst = HKg(g_st, (2 * h + kc) * 64, 64)
                    if first:
                        tk_up = OP("vector", lambda v, DK=DK, dst=dst: v.tensor_mul(
                            dst, DK, BCB), wait=[tk_bcb, tk_dk[kc]])
                        tk_dkr = tk_up
                    else:
                        tk_t1 = OP("vector", lambda v, kc=kc, h=h, DK=DK: v.tensor_add(
                            tmpb(0, 64), HKg(g_ld, (2 * h + kc) * 64, 64), DK),
                            wait=[tk_bcb, tk_dk[kc]])
                        tk_up = OP("vector", lambda v, dst=dst: v.tensor_mul(
                            dst, tmpb(0, 64), BCB), wait=[tk_t1])
                        tk_dkr = tk_t1
                    tk_hkbw = OP("vector", lambda v, kc=kc, h=h, dst=dst: v.tensor_copy(
                        HKB((2 * h + kc) * 64, 64), dst), wait=[tk_up])
                dkbf = tk_dkr
                dk_free = tk_hkbw
                tk_dv = None
                for jt in range(4):
                    tk_dv = MM(PS[4][0:64, 0:256], P_v(jt * 256 + h * 64, 64),
                               v_tm(jt * 1024 + h * 256, 256), jt == 0, jt == 3,
                               wait=[tk_P[jt], tk_vv[jt], dv_free, tk_dv,
                                     tk_bcb])
                dstv = HVg(g_st, h * 256, 256)
                if first:
                    tk_hv = OP("vector", lambda v, dstv=dstv, h=h: v.tensor_scalar_mul(
                        dstv, PS[4][0:64, 0:256], EBCC[0:64, h:h + 1]),
                        wait=[tk_dv, tk_ebcc])
                    ps4f = tk_hv
                else:
                    tk_hva = OP("vector", lambda v, h=h: v.scalar_tensor_tensor(
                        vf("tmpa", 0, 256, 0, 64), HVg(g_ld, h * 256, 256), 1.0,
                        PS[4][0:64, 0:256], ALU.mult, ALU.add),
                        wait=[tk_dv, tk_ebcc, prev_rms])
                    tk_hv = OP("vector", lambda v, dstv=dstv, h=h: v.tensor_scalar_mul(
                        dstv, vf("tmpa", 0, 256, 0, 64), EBCC[0:64, h:h + 1]),
                        wait=[tk_hva])
                    ps4f = tk_hva
                tk_hvbw = OP("vector", lambda v, dstv=dstv, h=h: v.tensor_copy(
                    HVB(h * 256, 256), dstv), wait=[tk_hv])
                if h % 2 == 1:
                    tk_hvbw = OP("vector", lambda v, dstv=dstv, h=h: v.tensor_copy(
                        HVB(h * 256, 256, 64), dstv), wait=[tk_hv, tk_hvbw])
                else:
                    tk_hvbw = OP("vector", lambda v, dstv=dstv, h=h: v.tensor_copy(
                        HVB(h * 256, 256, 64), dstv), wait=[tk_hv, tk_hvbw])
                dv_free = tk_hvbw
                tk_state_tail += [tk_hkbw, tk_hvbw]
            ps1_free = dkbf

        tk_o_all = tk_o_ev[(3, 3)]

        if not (STATE_EN and (lc < CHPS - 1 or STATE_FORCE)):
            ps1_free = sp_free[1]
        sdrop("st8")
        pbank = {"P7": P7, "PS0B": PS0B}
        prev_x1b = None
        ab_free = [l_free[0], l_free[1]]
        for tt in range(4):
            par = tt % 2
            ob_name = "P7" if par == 0 else "PS0B"
            OTB = pbank[ob_name]
            tk_pe = None
            for i in range(8):
                tk_pe = TRG(OTB[:, i * 128:(i + 1) * 128],
                            o_tmv(tt * 1024 + i * 128, 128), IDB, i == 0, i == 7,
                            wait=[tk_o_ev[(3, tt)], pfree[ob_name], tk_pe])
            tk_ofm = OP("vector", lambda v, OTB=OTB, par=par: v.tensor_copy(
                vb("o_fmt", par * 1024, 1024), OTB[:, :]),
                wait=[tk_pe, abank_free[par]])
            pfree[ob_name] = tk_ofm
            st8_ofm_last = tk_ofm
            AB = (PS[2], PS[3])
            tk_amm = {0: None, 1: None}
            for hvc in range(8):
                for half in range(2):
                    tk_amm[half] = MM(AB[half][:, 0:512],
                                      vb("o_fmt", par * 1024, 1024)[:, hvc * 128:(hvc + 1) * 128],
                                      WOv(hvc * 1024 + half * 512, 512),
                                      hvc == 0, hvc == 7,
                                      wait=[tk_ofm, tk_o_all, tk_amm[half],
                                            ab_free])
            OP("vector", lambda v, tt=tt, AB=AB: v.scalar_tensor_tensor(
                vf("x1pt", 0, 512), AB[0][:, 0:512], 1.0, x_tm(tt * 1024, 512),
                ALU.mult, ALU.add), wait=[tk_amm[0], prev_sq, prev_x1b, tk_x])
            tk_x1p = OP("vector", lambda v, tt=tt, AB=AB: v.scalar_tensor_tensor(
                vf("x1pt", 512, 512), AB[1][:, 0:512], 1.0,
                x_tm(tt * 1024 + 512, 512), ALU.mult, ALU.add),
                wait=[tk_amm[1]])
            abank_free[par] = tk_x1p
            ab_free = tk_x1p
            tk_sx = OP("vector", lambda v: v.reduce_sum(col(18), vf("x1pt", 0, 1024), axis=AX.X),
                       wait=[tk_x1p])
            tk_sq = OP("scalar", lambda s: s.activation(
                vb("x1bt", 0, 1024), vf("x1pt", 0, 1024), AF.Square,
                accum_out=col(19)),
                wait=[tk_x1p, prev_var, prev_pe2])
            prev_sq = tk_sq
            t_ = OP("vector", lambda v: v.tensor_scalar_mul(col(20), col(18), 1.0 / D), wait=[tk_sx])
            t_ = OP("vector", lambda v: v.tensor_mul(col(21), col(20), col(20)), wait=[t_])
            t2_ = OP("vector", lambda v: v.tensor_scalar_mul(col(22), col(19), 1.0 / D), wait=[tk_sq, t_])
            tk_var = OP("vector", lambda v: v.tensor_sub(col(22), col(22), col(21)), wait=[t2_])
            prev_var = tk_var
            tk_ln = OP("scalar", lambda s: s.activation(col(23), col(22), AF.Ln,
                                                        bias=EPS), wait=[tk_var, prev_x1])
            tk_rs = OP("scalar", lambda s: s.activation(col(24), col(23), AF.Exp,
                                                        scale=-0.5), wait=[tk_ln])
            tk_nm = OP("vector", lambda v: v.tensor_scalar_mul(col(25), col(20), -1.0),
                       wait=[tk_var])
            tk_nr = OP("vector", lambda v: v.tensor_mul(col(26), col(25), col(24)),
                       wait=[tk_rs, tk_nm])
            tk_x1 = OP("vector", lambda v: v.tensor_scalar(
                vf("x1tmt", 0, 1024), vf("x1pt", 0, 1024), col(24), col(26),
                ALU.mult, ALU.add), wait=[tk_nr, prev_st1])
            prev_x1 = tk_x1
            tk_x1b = OP("scalar", lambda s: s.activation(
                vb("x1bt", 0, 1024), vf("x1pt", 0, 1024), AF.Identity,
                scale=col(24), bias=col(26)),
                wait=[tk_nr, prev_pe2])
            prev_x1b = tk_x1b
            tk_st1 = OP("sync", lambda e, c=c, tt=tt: e.dma_start(
                out=x1tm_h.ap()[c * CA + tt * 128: c * CA + (tt + 1) * 128, :],
                in_=vf("x1tmt", 0, 1024)), wait=[tk_x1])
            prev_st1 = tk_st1
            x1b_name = "PS0B" if par == 0 else "P7"
            XTB = pbank[x1b_name]
            tk_pe2 = None
            for dc in range(8):
                tk_pe2 = TRG(XTB[:, dc * 128:(dc + 1) * 128],
                             vb("x1bt", 0, 1024)[:, dc * 128:(dc + 1) * 128], IDB,
                             dc == 0, dc == 7,
                             wait=[tk_x1b, pfree[x1b_name], tk_pe2])
            prev_pe2 = tk_pe2
            tk_x1f = OP("vector", lambda v, XTB=XTB: v.tensor_copy(
                vb("x1fmt", 0, 1024), XTB[:, :]), wait=[tk_pe2, prev_st2])
            pfree[x1b_name] = tk_x1f
            tk_st2 = OP("sync", lambda e, c=c, tt=tt: e.dma_start(
                out=x1fm_h.ap().rearrange("(dc p) n -> p dc n", p=128)[
                    :, :, c * CA + tt * 128: c * CA + (tt + 1) * 128],
                in_=vb("x1fmt", 0, 1024).rearrange("p (dc t) -> p dc t", dc=8)),
                wait=[tk_x1f])
            prev_st2 = tk_st2
        if c + 1 < NCHA:
            nsq, nlc = (c + 1) // CHPS, (c + 1) % CHPS
            nt0 = nlc * CA
            preload_x = []
            for tt in range(4):
                preload_x.append(OP("sync", lambda e, tt=tt, nsq=nsq, nt0=nt0:
                                    e.dma_start(
                    out=x_tm(tt * 1024, 1024),
                    in_=x_h.ap()[nsq, nt0 + tt * 128: nt0 + (tt + 1) * 128, :]),
                    wait=[abank_free[0], abank_free[1]]))
        prev_pe_tail = prev_pe2
        prev_state_done = tk_state_tail[-2:] if tk_state_tail else []
        if STATE_EN and (lc < CHPS - 1 or STATE_FORCE):
            prev_ebc_readers = [tk_bcb, tk_hv]
        else:
            prev_ebc_readers = []

        if debug and c == 0:
            OP("sync", lambda e: e.dma_start(out=dbg["df"].ap(), in_=f_v(0, 1024)), wait=[tk_ff[3]])
            OP("sync", lambda e: e.dma_start(out=dbg["deb"].ap(), in_=eb_v(0, 1024)), wait=[tk_eb[3]])
            OP("sync", lambda e: e.dma_start(out=dbg["dP"].ap(), in_=P_v(0, 1024)), wait=[tk_P[3]])

        if c == NCHA - 1:
            snap_ab = pl.snapshot()
            tk_b2pre = OP("gpsimd", lambda e: e.dma_start(
                out=vb("b2row", 0, 1024, 0, 1),
                in_=b2_h.ap().rearrange("(a f) -> a f", a=1)), wait=[prev_rms])
            w1g = []
            w1_waits = [tk_q_all, tk_k_all, tk_vv[3], abank_free[1]]
            for g in range(4):
                w1g.append(OP("gpsimd", lambda e, g=g: e.dma_start(
                    out=AW[:, g * 8192:(g + 1) * 8192].rearrange(
                        "p (dc f) -> p dc f", dc=2),
                    in_=w1_h.ap()[g * 256:(g + 1) * 256, :].rearrange(
                        "(dc p) f -> p dc f", p=128)), wait=[w1_waits[g]]))
            w2g = []
            w2_waits = [
                [l_free[0], l_free[1], tk_ktm],
                [tk_ktm, tk_o_ev[(3, 3)]],
                [st8_ofm_last, tk_eb[3], tk_P[3], prev_p2a, tk_ebcr],
                [l_free[0], l_free[1], tk_P[3], tk_pt2w[0], tk_pt2w[1],
                 prev_rms,
                 abank_free[0], st8_ofm_last],
            ]
            for gi in range(4):
                w2g.append(OP("gpsimd", lambda e, gi=gi: e.dma_start(
                    out=AC[:, gi * 8192:(gi + 1) * 8192].rearrange(
                        "p (fc d) -> p fc d", fc=8),
                    in_=w2_h.ap()[gi * 1024:(gi + 1) * 1024, :].rearrange(
                        "(fc p) d -> p fc d", p=128)), wait=w2_waits[gi]))
            pl.set_pending(snap_ab)

    pl.drop_te = TE_SELFWAIT_DROP
    W2C = lambda c0, n: vb("W2", c0, n)
    ACTB = lambda c0, n: vb("ACT", c0, n)
    B1R = lambda c0, n: vb("b1row", c0, n, 0, 1)
    B2R = lambda c0, n: vb("b2row", c0, n, 0, 1)
    YPT = vf("ypt", 0, 1024)
    YT = vf("yt", 0, 1024)
    W1v = lambda c0, n: AW[:, c0:c0 + n]
    XF = lambda c0, n: vb("x_fm", c0, n)
    X1T = lambda c0, n: vf("x_tm", c0, n)
    SIND, COSD = xb_v(0, 256), xb_v(256, 256)
    HPB, M1B = xb_v(512, 256), xb_v(768, 256)
    M2B, M3B = xb_v(1024, 256), xb_v(1280, 256)

    tk_b1l = OP("sync", lambda e: e.dma_start(
        out=vf("xb", 1024, 128, 0, 32),
        in_=b1_h.ap().rearrange("(fc p) -> fc p", p=128)))
    tk_b1t = TRG(PS[4][0:128, 0:32], vf("xb", 1024, 128, 0, 32),
                 IDF[0:32, 0:32], True, True, wait=[tk_b1l, tk_idf])
    tk_b1e = OP("vector", lambda v: v.tensor_copy(
        vf("b1c", 0, 32), PS[4][0:128, 0:32]), wait=[tk_b1t])
    tk_b1s = OP("vector", lambda v: v.tensor_scalar_mul(
        vf("b1c", 32, 32), vf("b1c", 0, 32), 0.1), wait=[tk_b1e])
    tk_b1 = OP("vector", lambda v: v.tensor_scalar_add(
        vf("b1c", 64, 32), vf("b1c", 32, 32), PI / 2), wait=[tk_b1s])
    tk_b2 = tk_b2pre
    tk_w1 = None
    tk_w2g = w2g

    hid_free = {0: None, 1: None, 2: tk_b1e}
    prev_m1 = prev_m2 = prev_m3 = None
    prev_act = {}
    prev_w2r = {}
    prev_yp = {0: [], 1: []}
    prev_hid_last = {0: None, 1: None}
    prev_x1t_readers = {0: [], 1: []}
    prev_ypt = {0: [], 1: []}

    def emit_bloads(c):
        cpar = c % 2
        xf0 = cpar * 2048
        xt0 = cpar * 2048
        t_xf = OP("sync", lambda e, c=c, xf0=xf0: e.dma_start(
            out=XF(xf0, 2048).rearrange("p (dc t) -> p dc t", dc=8),
            in_=x1fm_h.ap().rearrange("(dc p) n -> p dc n", p=128)[
                :, :, c * CB:(c + 1) * CB]), wait=[prev_hid_last[cpar]])
        t_x1 = OP("sync", lambda e, c=c, xt0=xt0: e.dma_start(
            out=X1T(xt0, 2048).rearrange("p (tt d) -> p tt d", tt=2),
            in_=x1tm_h.ap()[c * CB:(c + 1) * CB, :].rearrange(
                "(tt p) d -> p tt d", p=128)), wait=prev_x1t_readers[cpar])
        prev_x1t_readers[cpar] = []
        return t_xf, t_x1

    cur_loads = emit_bloads(0)
    for c in range(NCHB):
        sq, tloc = (c * CB) // T, (c * CB) % T
        cpar = c % 2
        xf0 = cpar * 2048
        xt0 = cpar * 2048
        tk_xf, tk_x1t = cur_loads
        ypsum = [(PS[5][:, 0:512], PS[6][:, 0:512]), (PS[0][:, 0:512], PS[1][:, 0:512])]
        tk_ymm = {}

        def emit_ypre():
            for tt2 in range(2):
                for dh in range(2):
                    tk_ymm[(tt2, dh)] = MM(ypsum[tt2][dh], ONES1B[0:1, 0:128],
                                           B2R(dh * 512, 512), True, False,
                                           wait=[tk_b2, tk_o1b] + prev_yp[tt2])
        w2pend = []
        tk_hid = {}

        def emit_w2(fc, tk_act):
            for tt2 in range(2):
                for dh in range(2):
                    tk_ymm[(tt2, dh)] = MM(
                        ypsum[tt2][dh], ACTB(fc * CB + tt2 * 128, 128),
                        W2C(fc * 1024 + dh * 512, 512), False, fc == 31,
                        wait=[tk_act, tk_w2g[fc // 8], tk_ymm[(tt2, dh)]])
            prev_w2r[fc] = tk_ymm[(1, 1)]
        for fc in range(32):
            hsl = fc % 3
            ph = PS[2 + hsl][:, 0:CB]
            tk_mm = None
            for dc in range(8):
                tk_mm = MM(ph, W1v(dc * 4096 + fc * 128, 128),
                           XF(xf0 + dc * CB, CB), dc == 0, dc == 7,
                           wait=[w1g[dc // 2], tk_xf, hid_free[hsl], tk_mm])
            tk_hid[fc] = tk_mm
            tk_sin = OP("scalar", lambda s, ph=ph, fc=fc: s.activation(
                SIND, ph, AF.Sin, scale=0.1, bias=vf("b1c", 32 + fc, 1)),
                wait=[tk_mm, tk_b1, prev_m1])
            tk_cos = OP("scalar", lambda s, ph=ph, fc=fc: s.activation(
                COSD, ph, AF.Sin, scale=0.1, bias=vf("b1c", 64 + fc, 1)),
                wait=[prev_m2, tk_b1])
            tk_hp = OP("scalar", lambda s2, ph=ph: s2.activation(
                HPB, ph, AF.Copy), wait=[tk_mm, prev_m3])
            hid_free[hsl] = tk_hp
            tk_m1 = OP("vector", lambda v, fc=fc: v.tensor_scalar_mul(
                M1B, SIND, vf("d2c", fc, 1)), wait=[tk_sin, te_d2, prev_m2])
            prev_m1 = tk_m1
            tk_m2 = OP("vector", lambda v, fc=fc: v.scalar_tensor_tensor(
                M2B, COSD, vf("d1c", fc, 1), M1B, ALU.mult, ALU.add),
                wait=[tk_cos, te_d1, tk_m1, prev_m3])
            prev_m2 = tk_m2
            tk_m3 = OP("vector", lambda v: v.tensor_add(M3B, M2B, HPB),
                       wait=[tk_hp, tk_m2, prev_act.get("last")])
            prev_m3 = tk_m3
            tk_act = OP("vector", lambda v, fc=fc: v.tensor_scalar(
                ACTB(fc * CB, CB), M3B, vf("b1c", fc, 1), 0.0,
                ALU.add, ALU.max),
                wait=[tk_m3, tk_b1, prev_w2r.get(fc)])
            prev_act["last"] = tk_act
            w2pend.append((fc, tk_act))
            if fc == 1:
                emit_ypre()
                prev_yp = {0: [], 1: []}
            if fc >= 2:
                emit_w2(*w2pend.pop(0))
        while w2pend:
            emit_w2(*w2pend.pop(0))
        prev_hid_last[cpar] = tk_hid[31]
        if c + 1 < NCHB:
            cur_loads = emit_bloads(c + 1)
        tk_ye = {}
        for tt2 in range(2):
            ya, yb = ypsum[tt2]
            tk_ya = OP("vector", lambda v, ya=ya, tt2=tt2, xt0=xt0:
                       v.scalar_tensor_tensor(
                           vf("b1row", tt2 * 1024, 512), ya, 1.0,
                           X1T(xt0 + tt2 * 1024, 512), ALU.mult, ALU.add),
                       wait=[tk_ymm[(tt2, 0)], tk_x1t] + prev_ypt[tt2])
            tk_yp = OP("vector", lambda v, yb=yb, tt2=tt2, xt0=xt0:
                       v.scalar_tensor_tensor(
                           vf("b1row", tt2 * 1024 + 512, 512), yb, 1.0,
                           X1T(xt0 + tt2 * 1024 + 512, 512), ALU.mult, ALU.add),
                       wait=[tk_ymm[(tt2, 1)], tk_ya])
            prev_yp[tt2] = [tk_yp]
            tk_ye[tt2] = tk_yp
        prev_x1t_readers[cpar].append(tk_ye[1])
        YPB = vf("b1row", 0, 2048)
        tk_sx = OP("vector", lambda v: v.reduce_sum(
            vf("cols", 28, 2), YPB.rearrange("p (g d) -> p g d", g=2),
            axis=AX.X), wait=[tk_ye[1]])
        tk_sq = {}
        for tt2 in range(2):
            tk_sq[tt2] = OP("scalar", lambda s, tt2=tt2: s.activation(
                tmpc(0, 1024), vf("b1row", tt2 * 1024, 1024), AF.Square,
                accum_out=col(30 + tt2)), wait=[tk_ye[tt2], tk_sq.get(0)])
        t_ = OP("vector", lambda v: v.tensor_scalar_mul(
            vf("cols", 32, 2), vf("cols", 28, 2), 1.0 / D), wait=[tk_sx])
        t_ = OP("vector", lambda v: v.tensor_mul(
            vf("cols", 34, 2), vf("cols", 32, 2), vf("cols", 32, 2)), wait=[t_])
        t2_ = OP("vector", lambda v: v.tensor_scalar(
            vf("cols", 36, 2), vf("cols", 30, 2), 1.0 / D, EPS,
            ALU.mult, ALU.add), wait=[tk_sq[0], tk_sq[1], t_])
        tk_vr = OP("vector", lambda v: v.tensor_sub(
            vf("cols", 36, 2), vf("cols", 36, 2), vf("cols", 34, 2)),
            wait=[t2_])
        tk_rsb = emit_rsqrt2(OP, 38, 36, 40, 2, wait=[tk_vr])
        t_ = OP("vector", lambda v: v.tensor_scalar_mul(
            vf("cols", 42, 2), vf("cols", 32, 2), -1.0), wait=[tk_vr])
        tk_nr = OP("vector", lambda v: v.tensor_mul(
            vf("cols", 42, 2), vf("cols", 42, 2), vf("cols", 38, 2)),
            wait=[tk_rsb, t_])
        for tt2 in range(2):
            tk_y = OP("vector", lambda v, tt2=tt2: v.tensor_scalar(
                vf("b1row", tt2 * 1024, 1024), vf("b1row", tt2 * 1024, 1024),
                col(38 + tt2), col(42 + tt2), ALU.mult, ALU.add),
                wait=[tk_nr])
            tk_st = OP("sync", lambda e, sq=sq, tloc=tloc, tt2=tt2:
                       e.dma_start(
                           out=y_h.ap()[sq, tloc + tt2 * 128:
                                        tloc + (tt2 + 1) * 128, :],
                           in_=vf("b1row", tt2 * 1024, 1024)), wait=[tk_y])
            prev_ypt[tt2] = [tk_st]

    import contextlib
    with contextlib.ExitStack() as stack:
        sems = {s_: stack.enter_context(nc.semaphore("sm_" + s_))
                for s_ in ALL_SEMS}
        block = stack.enter_context(nc.Block())

        @block.sync
        def _(e):
            pl.replay("sync", e, sems)
            pl.final_wait("sync", e, sems)

        @block.gpsimd
        def _(e):
            pl.replay("gpsimd", e, sems)

        @block.tensor
        def _(e):
            pl.replay("tensor", e, sems)

        @block.vector
        def _(e):
            pl.replay("vector", e, sems)

        @block.scalar
        def _(e):
            pl.replay("scalar", e, sems)

    return nc


def kernel(L0_attn_x, L0_attn_Wq, L0_attn_Wk, L0_attn_Wv, L0_attn_Wf,
           L0_attn_g_norm_w, L0_attn_Wo, L0_ffn_W1, L0_ffn_b1,
           L0_ffn_quantum_phases, L0_ffn_quantum_weights_real,
           L0_ffn_quantum_weights_imag, L0_ffn_W2, L0_ffn_b2):
    x = np.ascontiguousarray(np.asarray(L0_attn_x, dtype=np.float32))
    cm = np.ones((128, 128), np.float32)
    base = {
        "wq": np.ascontiguousarray(np.asarray(L0_attn_Wq, np.float32)),
        "wk": np.ascontiguousarray(np.asarray(L0_attn_Wk, np.float32)),
        "wv": np.ascontiguousarray(np.asarray(L0_attn_Wv, np.float32)),
        "wf": np.ascontiguousarray(np.asarray(L0_attn_Wf, np.float32)),
        "gnw": np.ascontiguousarray(np.asarray(L0_attn_g_norm_w, np.float32)),
        "wo": np.ascontiguousarray(np.asarray(L0_attn_Wo, np.float32)),
        "w1": np.ascontiguousarray(np.asarray(L0_ffn_W1, np.float32)),
        "b1": np.ascontiguousarray(np.asarray(L0_ffn_b1, np.float32)),
        "qph": np.ascontiguousarray(np.asarray(L0_ffn_quantum_phases, np.float32)),
        "qwr": np.ascontiguousarray(np.asarray(L0_ffn_quantum_weights_real, np.float32)),
        "qwi": np.ascontiguousarray(np.asarray(L0_ffn_quantum_weights_imag, np.float32)),
        "w2": np.ascontiguousarray(np.asarray(L0_ffn_W2, np.float32)),
        "b2": np.ascontiguousarray(np.asarray(L0_ffn_b2, np.float32)),
        "ctri": np.triu(cm).copy(),
        "cidb": np.eye(128, dtype=np.float32).astype(BFNP),
        "cidf": np.eye(128, dtype=np.float32),
        "cones": cm,
        "cones1b": np.ones((1, 512), np.float32).astype(BFNP),
        "cbias": np.broadcast_to(np.array([LNS, 1.0, EPS, PI / 2, MAGICR, 0, 0, 0],
                                          np.float32), (128, 8)).copy(),
    }
    nc = build_program(debug=False)
    in_maps = []
    for i in range(NCORES):
        m = dict(base)
        m["x"] = np.ascontiguousarray(x[i * SEQ:(i + 1) * SEQ])
        in_maps.append(m)
    res = run_bass_kernel_spmd(nc, in_maps, core_ids=list(range(NCORES)))
    return np.concatenate([r["y"] for r in res.results], axis=0)
